# version 04
# baseline (speedup 1.0000x reference)
import math
import numpy as np
import ml_dtypes
import concourse.bass as bass
import concourse.mybir as mybir
from concourse.bass_utils import run_bass_kernel_spmd

F32 = mybir.dt.float32
BF16 = mybir.dt.bfloat16
U8 = mybir.dt.uint8
AF = mybir.ActivationFunctionType
ALU = mybir.AluOpType
AX = mybir.AxisListType

D = 2048
T = 2048
H = 16
G4 = 4
DK = 128
DFF = 1024
NE = 8
NDT = 16
NTT = 16
EPS = 1e-5
ISQDK = 1.0 / math.sqrt(DK)

_DEBUG = []

OFF_IDEN = 0
OFF_ONES = 256
OFF_MASK = 512
OFF_RWT = 4608
OFF_CGATE = 4864
OFF_STAT = 5120
OFF_EPS = 5376
OFF_DEN = 5440
OFF_DENB = 7488
OFF_RB = 8512
OFF_A = 12672
OFF_B = OFF_A + 65536
OFF_C = OFF_B + 65536
OFF_C2 = OFF_C + 32768
ARENA_BYTES = OFF_C + 65536


class Prog:
    ENGINES = ("tensor", "vector", "scalar", "sync", "gpsimd")
    SEMS = ("T", "V", "S", "G", "Y", "ST")

    def __init__(self):
        self.ops = {e: [] for e in self.ENGINES}
        self.cnt = {s: 0 for s in self.SEMS}

    def add(self, eng, kind, args, waits=None, inc=None, n=1):
        w = {k: v for k, v in (waits or {}).items() if v and v > 0}
        mile = None
        if inc is not None:
            self.cnt[inc] += n
            mile = self.cnt[inc]
        self.ops[eng].append((w, inc, n, kind, args))
        return mile

    def mm(self, out, lhsT, rhs, start, stop, waits=None, inc=None):
        return self.add("tensor", "mm", (out, lhsT, rhs, start, stop), waits, inc)

    def tr(self, out, in_, iden, waits=None, inc=None):
        return self.add("tensor", "tr", (out, in_, iden), waits, inc)

    def tt(self, out, in0, in1, op, waits=None, inc=None):
        return self.add("vector", "tt", (out, in0, in1, op), waits, inc)

    def ts(self, out, in0, s1, s2, op0, op1=None, waits=None, inc=None):
        return self.add("vector", "ts", (out, in0, s1, s2, op0, op1), waits, inc)

    def stt(self, out, in0, sc, in1, op0, op1, waits=None, inc=None):
        return self.add("vector", "stt", (out, in0, sc, in1, op0, op1), waits, inc)

    def red(self, out, in_, op, waits=None, inc=None):
        return self.add("vector", "red", (out, in_, op), waits, inc)

    def vcp(self, out, in_, waits=None, inc=None):
        return self.add("vector", "cp", (out, in_), waits, inc)

    def recip(self, out, in_, waits=None, inc=None):
        return self.add("vector", "recip", (out, in_), waits, inc)

    def vself(self):
        m = self.add("vector", "nopinc", (), None, "V")
        self.add("vector", "wait", (), {"V": m})

    def act(self, out, in_, func, scale=None, bias=None, accum=None, waits=None, inc=None):
        return self.add("scalar", "act", (out, in_, func, scale, bias, accum), waits, inc)

    def scp(self, out, in_, waits=None, inc=None):
        return self.add("scalar", "cp", (out, in_), waits, inc)

    def dma(self, q, out, in_, waits=None, inc=None):
        return self.add(q, "dma", (out, in_), waits, inc, n=16)

    def mset(self, ap, val, waits=None, inc=None):
        return self.add("vector", "mset", (ap, val), waits, inc)

    def fin(self, eng, waits):
        self.add(eng, "wait", (), waits)

    def replay(self, eng, name, sems):
        for (w, inc, n, kind, a) in self.ops[name]:
            for s, v in w.items():
                eng.wait_ge(sems[s], v)
            r = None
            if kind == "mm":
                r = eng.matmul(a[0], a[1], a[2], start=a[3], stop=a[4])
            elif kind == "tr":
                r = eng.transpose(a[0], a[1], a[2])
            elif kind == "tt":
                r = eng.tensor_tensor(a[0], a[1], a[2], op=a[3])
            elif kind == "ts":
                r = eng.tensor_scalar(a[0], a[1], a[2], a[3], op0=a[4],
                                      op1=(a[5] if a[5] is not None else ALU.bypass))
            elif kind == "stt":
                r = eng.scalar_tensor_tensor(a[0], a[1], a[2], a[3], op0=a[4], op1=a[5])
            elif kind == "red":
                r = eng.tensor_reduce(a[0], a[1], axis=AX.X, op=a[2])
            elif kind == "cp":
                r = eng.tensor_copy(a[0], a[1]) if name == "vector" else eng.copy(a[0], a[1])
            elif kind == "recip":
                r = eng.reciprocal(a[0], a[1])
            elif kind == "act":
                kwargs = {}
                if a[3] is not None:
                    kwargs["scale"] = a[3]
                if a[4] is not None:
                    kwargs["bias"] = a[4]
                if a[5] is not None:
                    kwargs["accum_out"] = a[5]
                r = eng.activation(a[0], a[1], a[2], **kwargs)
            elif kind == "dma":
                r = eng.dma_start(out=a[0], in_=a[1])
            elif kind == "mset":
                r = eng.memset(a[0], a[1])
            elif kind == "nopinc":
                r = None
            elif kind == "wait":
                continue
            if inc is not None:
                if r is not None:
                    r.then_inc(sems[inc], n)
                else:
                    eng.sem_inc(sems[inc], n)


def _r(ap, pattern, **kw):
    return ap.rearrange(pattern, **kw)


def build_program(debug_taps=(), stage=4):
    nc = bass.Bass("TRN2", target_bir_lowering=False)

    x_in = nc.dram_tensor("x", [T, D], F32, kind="ExternalInput").ap()
    xT_in = nc.dram_tensor("xT", [128, NDT, T], F32, kind="ExternalInput").ap()
    qwT_in = nc.dram_tensor("qwT", [16, 128, NDT, 128], F32, kind="ExternalInput").ap()
    kwT_in = nc.dram_tensor("kwT", [4, 128, NDT, 128], F32, kind="ExternalInput").ap()
    vwT_in = nc.dram_tensor("vwT", [128, NDT, 512], F32, kind="ExternalInput").ap()
    cwT_in = nc.dram_tensor("cwT", [128, NDT, D], F32, kind="ExternalInput").ap()
    rwT_in = nc.dram_tensor("rwT", [128, NDT, 8], F32, kind="ExternalInput").ap()
    w1_in = nc.dram_tensor("w1", [NE, 128, NDT, DFF], F32, kind="ExternalInput").ap()
    w2_in = nc.dram_tensor("w2", [NE, 128, NDT, DFF], F32, kind="ExternalInput").ap()
    w3_in = nc.dram_tensor("w3", [NE, 128, 8, D], F32, kind="ExternalInput").ap()
    iden_in = nc.dram_tensor("iden", [128, 128], BF16, kind="ExternalInput").ap()
    ones_in = nc.dram_tensor("ones", [128, 128], BF16, kind="ExternalInput").ap()
    mask_in = nc.dram_tensor("masks", [128, 4, 512], BF16, kind="ExternalInput").ap()
    ropeC_in = nc.dram_tensor("ropeC", [128, T], F32, kind="ExternalInput").ap()
    ropeS_in = nc.dram_tensor("ropeS", [128, T], F32, kind="ExternalInput").ap()
    y_out = nc.dram_tensor("y", [T, D], F32, kind="ExternalOutput").ap()
    x1f_d = nc.dram_tensor("x1f", [NTT, 128, D], F32, kind="Internal").ap()
    taps = {}
    for t_ in debug_taps:
        shp = {"qt": [128, H * T], "kt": [128, G4 * T], "v": [128, NTT * 512],
               "ot": [128, H * T], "x1": [T, D], "expt": [128, 16 * 512],
               "den": [128, 512], "rbs": [128, 512], "cg": [128, 64],
               "hbuf": [128, 8 * 1024]}[t_]
        dt_ = F32 if t_ in ("x1", "den", "rbs", "cg") else BF16
        taps[t_] = nc.dram_tensor("tap_" + t_, shp, dt_, kind="ExternalOutput").ap()

    from contextlib import ExitStack
    with ExitStack() as ctx:
        en = ctx.enter_context
        arena = en(nc.sbuf_tensor("arena", [128, ARENA_BYTES], U8))
        af = arena.bitcast(F32)
        ab = arena.bitcast(BF16)
        pbig = en(nc.psum_tensor("pbig", [128, 4096], F32))
        pbig_bf = pbig.bitcast(BF16)

        def fsl(off, n, p=slice(None)):
            return af[p, off // 4: off // 4 + n]

        def bsl(off, n, p=slice(None)):
            return ab[p, off // 2: off // 2 + n]

        iden = bsl(OFF_IDEN, 128)
        onesb = bsl(OFF_ONES, 128)
        scrS = fsl(5392, 2, p=slice(0, 1))
        scrV = fsl(5408, 2, p=slice(0, 1))
        masks = bsl(OFF_MASK, 4 * 512)
        rwT = bsl(OFF_RWT, 16 * 8)
        rwT3 = _r(rwT, "p (dt e) -> p dt e", e=8)
        cgate = fsl(OFF_CGATE, 64)
        stat = fsl(OFF_STAT, 64)
        epsv = fsl(OFF_EPS, 1)
        sden = fsl(OFF_DEN, 512, p=slice(0, 1))
        sdenb = bsl(OFF_DENB, 512, p=slice(0, 1))
        rb = [fsl(OFF_RB, 512), fsl(OFF_RB + 2048, 512)]
        A_bf = bsl(OFF_A, 32768)
        A_xT = _r(A_bf, "p (dt t) -> p dt t", t=T)
        A_cwT = _r(A_bf, "p (dt o) -> p dt o", o=D)
        A_x1T = _r(bsl(OFF_A, 16 * 1024), "p (dt t) -> p dt t", t=1024)
        A_silu = bsl(OFF_A + 32768, 8 * 1024)
        A_h = bsl(OFF_A + 49152, 8 * 1024)
        B_bf = bsl(OFF_B, 32768)
        wbuf = [_r(bsl(OFF_B, 16 * 1024), "p (dt f) -> p dt f", f=1024),
                _r(bsl(OFF_B + 32768, 16 * 1024), "p (dt f) -> p dt f", f=1024)]
        KT = bsl(OFF_C, 4 * T)
        Vt = bsl(OFF_C + 16384, 16 * 512)
        ropeC = fsl(OFF_C2, T)
        ropeS = fsl(OFF_C2 + 8192, T)
        tmp1 = fsl(OFF_C2 + 16384, 1024)
        tmp2 = fsl(OFF_C2 + 20480, 1024)
        wstream = [bsl(OFF_C2 + 24576, 2048), bsl(OFF_C2 + 28672, 2048)]
        vwTb = _r(bsl(OFF_C2 + 16384, 16 * 512), "p (dt o) -> p dt o", o=512)
        if globals().get("_DBG_RING", False):
            expT = bsl(OFF_A + 32768, 16 * 512)
        else:
            expT = bsl(OFF_C2, 16 * 512)
        xres = [fsl(OFF_C2 + 16384, 2048), fsl(OFF_C2 + 24576, 2048)]
        sscr = fsl(OFF_C + 16384, 2048)
        lndump = fsl(OFF_C + 24576, 2048)
        x1b = bsl(OFF_C2 + 16384, 2048)
        x1stage = [bsl(OFF_C2 + 20480, 2048), bsl(OFF_C2 + 24576, 2048)]
        acc = fsl(OFF_C, 8 * 2048)
        ln2dump = fsl(OFF_MASK, 1024)

        P = Prog()

        def vflush(src):
            return P.vcp(scrV[:, 0:1], src[0:1, 0:1], inc="V")

        def sflush(src):
            return P.scp(scrS[:, 0:1], src[0:1, 0:1], inc="S")

        def _finish():
            P.fin("sync", {"ST": P.cnt["ST"]})
            sems = {}
            for s in Prog.SEMS:
                sems[s] = en(nc.semaphore("sem" + s))
            block = en(nc.Block())

            @block.tensor
            def _(eng):
                P.replay(eng, "tensor", sems)

            @block.vector
            def _(eng):
                P.replay(eng, "vector", sems)

            @block.scalar
            def _(eng):
                P.replay(eng, "scalar", sems)

            @block.sync
            def _(eng):
                P.replay(eng, "sync", sems)

            @block.gpsimd
            def _(eng):
                P.replay(eng, "gpsimd", sems)

        P.dma("sync", iden, iden_in, inc="Y")
        P.dma("sync", onesb, ones_in, inc="Y")
        P.dma("sync", _r(masks, "p (a b) -> p a b", a=4), mask_in, inc="Y")
        P.dma("sync", ropeC, ropeC_in, inc="Y")
        yconst = P.dma("sync", ropeS, ropeS_in, inc="Y")
        P.mset(epsv, EPS, inc="V")

        g_rwT = P.dma("gpsimd", rwT3, rwT_in, inc="G")
        g_w0 = P.dma("gpsimd",
                     _r(wstream[0], "p (dt o) -> p dt o", o=128),
                     _r(qwT_in[0:1], "a p dt o -> p (a dt) o"), inc="G")
        g_xT = P.dma("gpsimd", A_xT, xT_in, inc="G")
        P.dma("gpsimd", bsl(OFF_B, 2048), xT_in[:, 0, 0:2048], inc="G")
        g_canary = P.dma("gpsimd", fsl(OFF_DEN, 16), xT_in[:, 0, 0:16], inc="G")

        NQO = 20
        g_wtile = {0: g_w0}
        proj_done = {}
        rope_evac = {}
        for ot in range(NQO):
            if ot > 0:
                src = qwT_in[ot:ot + 1] if ot < 16 else kwT_in[ot - 16:ot - 15]
                w = {"T": proj_done.get(ot - 2, 0)} if ot >= 2 else None
                g_wtile[ot] = P.dma("gpsimd",
                                    _r(wstream[ot % 2], "p (dt o) -> p dt o", o=128),
                                    _r(src, "a p dt o -> p (a dt) o"), waits=w, inc="G")
            par = ot % 2
            pq = pbig[:, par * 2048:(par + 1) * 2048]
            buf3 = _r(wstream[ot % 2], "p (dt o) -> p dt o", o=128)
            gw = max(g_xT, g_wtile[ot])
            if ot == 0 and g_canary:
                gw = max(gw, g_canary)
            w0 = {"G": gw, "V": rope_evac.get(ot - 2, 0)}
            for dt in range(NDT):
                for nch in range(4):
                    last = (dt == NDT - 1 and nch == 3)
                    P.mm(pq[:, nch * 512:(nch + 1) * 512], buf3[:, dt, :],
                         A_xT[:, dt, nch * 512:(nch + 1) * 512],
                         start=(dt == 0), stop=(dt == NDT - 1),
                         waits=(w0 if (dt == 0 and nch == 0) else None),
                         inc=("T" if last else None))
            proj_done[ot] = P.cnt["T"]
            dst = bsl(OFF_B + ot * 4096, T) if ot < 16 else bsl(OFF_C + (ot - 16) * 4096, T)
            wv = {"T": proj_done[ot], "Y": yconst}
            for ch in range(2):
                cs = slice(ch * 1024, (ch + 1) * 1024)
                P.tt(tmp1, pq[:, cs], ropeC[:, cs], ALU.mult, waits=(wv if ch == 0 else None))
                P.tt(tmp2[0:64, :], pq[64:128, cs], ropeS[0:64, cs], ALU.mult)
                P.tt(tmp2[64:128, :], pq[0:64, cs], ropeS[64:128, cs], ALU.mult)
                P.vself()
                P.tt(dst[:, cs], tmp1, tmp2, ALU.add, inc=("V" if ch == 1 else None))
            rope_evac[ot] = P.cnt["V"]
        rope_all = P.cnt["V"]

        P.dma("gpsimd", vwTb, vwT_in, waits={"V": rope_evac[19]}, inc="G")
        P.dma("gpsimd", bsl(OFF_DENB, 512), vwT_in[:, 0, 0:512], inc="G")
        g_vwT = P.dma("gpsimd", fsl(OFF_DEN, 16), vwT_in[:, 0, 0:16], inc="G")
        vevac = {}
        for tt in range(NTT):
            pv = pbig[:, (tt % 2) * 2048:(tt % 2) * 2048 + 512]
            w0 = {"G": g_vwT, "V": rope_all, "S": vevac.get(tt - 2, 0)}
            for dt in range(NDT):
                P.mm(pv, A_xT[:, dt, tt * 128:(tt + 1) * 128], vwTb[:, dt, :],
                     start=(dt == 0), stop=(dt == NDT - 1),
                     waits=(w0 if dt == 0 else None),
                     inc=("T" if dt == NDT - 1 else None))
            m = P.cnt["T"]
            vevac[tt] = P.scp(Vt[:, tt * 512:(tt + 1) * 512], pv, waits={"T": m}, inc="S")
        p1_T = P.cnt["T"]
        p1_S = P.cnt["S"]

        if "qt" in taps:
            P.dma("sync", taps["qt"], B_bf, waits={"V": rope_all}, inc="ST")
        if "kt" in taps:
            P.dma("sync", taps["kt"], KT, waits={"V": rope_all}, inc="ST")
        if "v" in taps:
            P.dma("sync", taps["v"], Vt, waits={"S": p1_S}, inc="ST")

        if stage < 2:
            _finish()
            return nc

        g_cwT = P.dma("gpsimd", A_cwT, cwT_in, waits={"T": p1_T}, inc="G")

        prev_evac = [0, 0]
        prev_rbm = [0, 0]
        lastexp = [0, 0, 0, 0]
        exp_mile = {}
        pend = {}

        def flush_tail():
            if not pend:
                return
            par, h_, qs_, pv_stop, exp_last, recip_m = (
                pend["par"], pend["h"], pend["qs"],
                pend["pv_stop"], pend["exp_last"], pend["recip_m"])
            dbank = pbig[:, (6 + par) * 512: (7 + par) * 512]
            obank = pbig[:, (4 + par) * 512: (5 + par) * 512]
            bc_m = P.mm(dbank, onesb[0:1, :], sdenb, True, True,
                        waits={"V": recip_m, "S": exp_last}, inc="T")
            P.scp(rb[par], dbank, waits={"T": bc_m, "V": prev_evac[par]}, inc="S")
            prev_rbm[par] = sflush(rb[par])
            prev_evac[par] = P.tt(bsl(OFF_B + h_ * 4096, T)[:, qs_], obank, rb[par],
                                  ALU.mult, waits={"S": prev_rbm[par], "T": pv_stop},
                                  inc="V")
            pend.clear()

        _H2 = globals().get("_P2H", H)
        _C2n = globals().get("_P2C", 4)
        _horder = globals().get("_HORDER", None) or list(range(_H2))
        first_q = True
        for h in _horder:
            g = h // 4
            QTh = bsl(OFF_B + h * 4096, T)
            for c in range(_C2n):
                par = (h * 4 + c) % 2
                jmax = 4 * c + 3
                qs = slice(c * 512, (c + 1) * 512)
                lt_mile = {}
                mask_last = 0
                lt_bank = lambda j: pbig[:, (j % 4) * 512: (j % 4) * 512 + 512]
                esl = lambda j: expT[:, j * 512:(j + 1) * 512]
                obank = pbig[:, (4 + par) * 512: (5 + par) * 512]
                dbank = pbig[0:1, (6 + par) * 512: (7 + par) * 512]

                def lt(j):
                    w = {"S": lastexp[j % 4]}
                    if first_q and j == 0:
                        w["V"] = rope_all
                        w["S"] = max(w["S"], p1_S)
                    lt_mile[j] = P.mm(lt_bank(j),
                                      KT[:, g * T + j * 128: g * T + (j + 1) * 128],
                                      QTh[:, qs], True, True, waits=w, inc="T")

                def den_pv(j, dwait):
                    if j == 0:
                        dwait = dict(dwait)
                        dwait["S"] = max(dwait.get("S", 0), prev_rbm[par])
                        dwait["V"] = max(dwait.get("V", 0), prev_evac[par])
                    P.mm(dbank, onesb[:, 0:1], esl(j), j == 0, j == jmax,
                         waits=dwait, inc=("T" if j == jmax else None))
                    P.mm(obank, Vt[:, j * 512 + g * 128: j * 512 + g * 128 + 128],
                         esl(j), j == 0, j == jmax,
                         inc=("T" if j == jmax else None))

                lt(0)
                first_q = False
                for j in range(jmax + 1):
                    if j + 1 <= jmax:
                        lt(j + 1)
                    exp_mile[(h, c, j)] = P.act(esl(j), lt_bank(j),
                                                AF.Exp, scale=ISQDK,
                                                waits={"T": lt_mile[j]}, inc="S")
                    lastexp[j % 4] = exp_mile[(h, c, j)]
                    if j >= 4 * c:
                        mask_last = P.tt(esl(j), esl(j),
                                         masks[:, (j - 4 * c) * 512:(j - 4 * c + 1) * 512],
                                         ALU.mult, waits={"S": exp_mile[(h, c, j)]}, inc="V")
                    if j >= 1:
                        den_pv(j - 1, {"S": exp_mile[(h, c, j)], "V": mask_last})
                    if j == 3:
                        flush_tail()
                sf = sflush(esl(jmax))
                vf = vflush(esl(jmax))
                den_pv(jmax, {"S": sf, "V": vf})
                pv_stop = P.cnt["T"]
                den_stop = pv_stop - 1
                P.vcp(sden, dbank, waits={"T": den_stop}, inc="V")
                P.add("vector", "wait", (), {"V": P.cnt["V"]})
                P.recip(sden, sden, inc="V")
                P.add("vector", "wait", (), {"V": P.cnt["V"]})
                P.vcp(sdenb, sden, inc="V")
                P.add("vector", "wait", (), {"V": P.cnt["V"]})
                pend.update(par=par, h=h, qs=qs, pv_stop=pv_stop,
                            exp_last=exp_mile[(h, c, jmax)], recip_m=vflush(sdenb))
        flush_tail()
        p2_T = P.cnt["T"]
        vflush(bsl(OFF_B + (H - 1) * 4096, T))
        p2_V = P.cnt["V"]


        if "ot" in taps:
            P.dma("sync", taps["ot"], B_bf, waits={"V": p2_V}, inc="ST")
        if "expt" in taps:
            P.dma("sync", taps["expt"], expT, waits={"V": p2_V, "S": P.cnt["S"]}, inc="ST")
        if "den" in taps:
            P.dma("sync", taps["den"], fsl(OFF_DEN, 512), waits={"V": p2_V}, inc="ST")
        if "rbs" in taps:
            P.dma("sync", taps["rbs"], rb[0], waits={"V": p2_V, "S": P.cnt["S"]}, inc="ST")
        if stage < 3:
            _finish()
            return nc

        xres_mile = {}
        resid_add = {}
        norm_mile = {}
        x1b_mile = {}
        ttev = {}
        tr_done = {}
        x1f_store = {}
        stage_store = {}
        for tt in range(NTT):
            s3 = tt % 2
            base = s3 * 2048
            w = {"T": p1_T} if tt < 2 else {"V": resid_add[tt - 2]}
            xres_mile[tt] = P.dma("sync", xres[tt % 2], x_in[tt * 128:(tt + 1) * 128, :],
                                  waits=w, inc="Y")
            w0 = {"G": g_cwT, "V": p2_V}
            if tt >= 2:
                w0["V"] = max(w0["V"], resid_add[tt - 2])
            for dt in range(NDT):
                for nch in range(4):
                    last = (dt == NDT - 1 and nch == 3)
                    P.mm(pbig[:, base + nch * 512: base + (nch + 1) * 512],
                         bsl(OFF_B + dt * 4096, T)[:, tt * 128:(tt + 1) * 128],
                         A_cwT[:, dt, nch * 512:(nch + 1) * 512],
                         start=(dt == 0), stop=(dt == NDT - 1),
                         waits=(w0 if (dt == 0 and nch == 0) else None),
                         inc=("T" if last else None))
            cp_m = P.cnt["T"]
            P.tt(sscr, pbig[:, base: base + 2048], xres[tt % 2], ALU.add,
                 waits={"T": cp_m, "Y": max(xres_mile[tt],
                                            x1f_store.get(tt - 1, 0))},
                 inc="V")
            resid_add[tt] = P.cnt["V"]
            ra_f = vflush(sscr)
            sc = (tt % 4) * 16
            P.act(lndump, sscr, AF.Identity, accum=stat[:, sc:sc + 1],
                  waits={"V": ra_f})
            st_m = P.act(lndump, sscr, AF.Square, accum=stat[:, sc + 1:sc + 2], inc="S")
            st_m = sflush(stat[:, sc:sc + 2])
            P.ts(stat[:, sc + 2:sc + 3], stat[:, sc:sc + 1], 1.0 / D, None, ALU.mult,
                 waits={"S": st_m})
            P.ts(stat[:, sc + 3:sc + 4], stat[:, sc + 1:sc + 2], 1.0 / D, None, ALU.mult)
            P.vself()
            P.tt(stat[:, sc + 4:sc + 5], stat[:, sc + 2:sc + 3], stat[:, sc + 2:sc + 3], ALU.mult)
            P.vself()
            P.tt(stat[:, sc + 5:sc + 6], stat[:, sc + 3:sc + 4], stat[:, sc + 4:sc + 5],
                 ALU.subtract)
            var_m = vflush(stat[:, sc + 5:sc + 6])
            P.act(stat[:, sc + 6:sc + 7], stat[:, sc + 5:sc + 6], AF.Sqrt, bias=epsv,
                  waits={"V": var_m}, inc="S")
            sq_m = sflush(stat[:, sc + 6:sc + 7])
            P.recip(stat[:, sc + 7:sc + 8], stat[:, sc + 6:sc + 7], waits={"S": sq_m})
            P.vself()
            norm_mile[tt] = P.ts(sscr, sscr, stat[:, sc + 2:sc + 3], stat[:, sc + 7:sc + 8],
                                 ALU.subtract, ALU.mult, inc="V")
            nf = vflush(sscr)
            x1f_store[tt] = P.dma("sync", x1f_d[tt], sscr, waits={"V": nf}, inc="Y")
            if "x1" in taps:
                P.dma("sync", taps["x1"][tt * 128:(tt + 1) * 128, :], sscr, inc="ST")
        p3_T = P.cnt["T"]
        p3_V = P.cnt["V"]
        p3_S = P.cnt["S"]
        if stage < 4:
            _finish()
            return nc

        wload = {}
        sweep_done = {}
        h_mile_prev = {}
        prev_silu7 = 0
        p4v_prev = 0
        y_st_total = 0
        for hf in range(2):
            wacc = {"V": p3_V, "S": p3_S, "T": p3_T} if hf == 0 else \
                {"V": p4v_prev, "ST": y_st_total, "T": sweep_done.get(hf * 24 - 1, 0)}
            P.dma("sync", _r(acc, "p (tt d) -> p tt d", d=D),
                  _r(x1f_d[hf * 8:(hf + 1) * 8], "tt p d -> p tt d"),
                  waits=wacc, inc="Y")
            P.dma("sync", fsl(OFF_DEN, 512), x1f_d[0][0:128, 0:512], inc="Y")
            acc_ld = P.dma("sync", fsl(OFF_DEN, 16), x1f_d[0][0:128, 0:16], inc="Y")
            x1b4 = bsl(OFF_A + 32768, 2048)
            trev = {}
            tr_t = {}
            for tt in range(8):
                s4 = tt % 2
                wv = {"Y": acc_ld}
                if tt >= 1:
                    wv["T"] = tr_t[tt - 1]
                P.vcp(x1b4, acc[:, tt * 2048:(tt + 1) * 2048], waits=wv, inc="V")
                x1b4_m = vflush(x1b4)
                for dt in range(NDT):
                    bko = (s4 * 2048 + (dt // 4) * 512) * 2 + (dt % 4) * 128
                    w = None
                    if dt == 0:
                        w = {"V": x1b4_m}
                        if tt >= 2:
                            w["S"] = trev[(tt - 2, 3)]
                    P.tr(pbig_bf[:, bko: bko + 128], x1b4[:, dt * 128:(dt + 1) * 128],
                         iden, waits=w, inc=("T" if dt % 4 == 3 else None))
                    if dt % 4 == 3:
                        grp = dt // 4
                        trev[(tt, grp)] = P.scp(
                            A_x1T[:, grp * 4:(grp + 1) * 4, tt * 128:(tt + 1) * 128],
                            _r(pbig_bf[:, (s4 * 2048 + grp * 512) * 2:
                                       (s4 * 2048 + grp * 512) * 2 + 512],
                               "p (a b) -> p a b", b=128),
                            waits={"T": P.cnt["T"]}, inc="S")
                tr_t[tt] = P.cnt["T"]
            x1t_gate = sflush(bsl(OFF_A, 4))
            x1t_ld = acc_ld
            def emit_wload(k):
                e_, m_ = (k % 24) // 3, (k % 24) % 3
                src = (w1_in, w2_in, w3_in)[m_]
                pat = "a p dt f -> p (a dt) f" if m_ < 2 else "a p ft d -> p (a ft) d"
                s_ap = _r(src[e_:e_ + 1], pat)
                w = {"T": sweep_done.get(k - 2, p3_T if k < 2 else 0)}
                dst = wbuf[k % 2] if m_ < 2 else \
                    _r(bsl(OFF_B + (k % 2) * 32768, 16 * 1024), "p (ft d) -> p ft d", d=D)
                wload[k] = P.dma("gpsimd", dst, s_ap, waits=w, inc="G")
                if k % 24 < 3:
                    P.dma("gpsimd", bsl(OFF_DENB, 512), xT_in[:, 0, 0:512], inc="G")
                    wload[k] = P.dma("gpsimd", fsl(OFF_DEN, 16),
                                     xT_in[:, 0, 0:16], inc="G")
                return wload[k]

            emit_wload(hf * 24 + 0)
            emit_wload(hf * 24 + 1)
            lgcp = {}
            cg_mile = {}
            lg = stat[:, 16:24]
            prt = pbig[:, 2048:2056]
            for ht in range(8):
                w0 = {"Y": x1t_ld, "G": g_rwT, "V": lgcp.get(ht - 1, p4v_prev),
                      "S": x1t_gate}
                for dt in range(NDT):
                    P.mm(prt, A_x1T[:, dt, ht * 128:(ht + 1) * 128], rwT3[:, dt, :],
                         start=(dt == 0), stop=(dt == NDT - 1),
                         waits=(w0 if dt == 0 else None),
                         inc=("T" if dt == NDT - 1 else None))
                rt_m = P.cnt["T"]
                lgcp[ht] = P.vcp(lg, prt, waits={"T": rt_m}, inc="V")
                P.vself()
                P.red(stat[:, 24:25], lg, ALU.max)
                P.vself()
                P.ts(stat[:, 32:40], lg, stat[:, 24:25], None, ALU.is_ge)
                P.vself()
                P.stt(stat[:, 40:48], stat[:, 32:40], -1e30, lg, ALU.mult, ALU.add)
                P.vself()
                P.red(stat[:, 25:26], stat[:, 40:48], ALU.max)
                P.vself()
                P.ts(stat[:, 48:56], stat[:, 40:48], stat[:, 25:26], None, ALU.is_ge)
                P.tt(stat[:, 26:27], stat[:, 24:25], stat[:, 25:26], ALU.subtract)
                d12_m = vflush(stat[:, 26:27])
                P.act(stat[:, 27:28], stat[:, 26:27], AF.Exp, scale=-1.0,
                      waits={"V": d12_m}, inc="S")
                e12_m = sflush(stat[:, 27:28])
                P.ts(stat[:, 28:29], stat[:, 27:28], 1.0, None, ALU.add, waits={"S": e12_m})
                P.vself()
                P.recip(stat[:, 29:30], stat[:, 28:29])
                P.vself()
                P.ts(stat[:, 30:31], stat[:, 29:30], -1.0, 1.0, ALU.mult, ALU.add)
                P.vself()
                P.ts(stat[:, 56:64], stat[:, 48:56], stat[:, 30:31], None, ALU.mult)
                cg_mile[ht] = P.stt(cgate[:, ht * 8:(ht + 1) * 8], stat[:, 32:40],
                                    stat[:, 29:30], stat[:, 56:64], ALU.mult, ALU.add,
                                    inc="V")
            accev = {}
            _NE = globals().get("_P4E", NE)
            for e in range(_NE):
                k1, k2, k3 = hf * 24 + 3 * e, hf * 24 + 3 * e + 1, hf * 24 + 3 * e + 2
                silu_mile = {}
                v_mile = {}
                h_mile = {}
                g_mile = {}
                for ft in range(8):
                    par = ft % 2
                    pg = pbig[:, par * 1024:(par + 1) * 1024]
                    sgate = silu_mile.get(ft - 2, 0)
                    if ft < 2:
                        sgate = max(sgate, prev_silu7)
                        if e == 0:
                            sgate = max(sgate, x1t_gate)
                    w0 = {"G": wload[k1], "S": sgate, "V": cg_mile[7]}
                    if e > 0:
                        w0["V"] = max(w0["V"], accev.get((e - 1, 7), 0))
                    for dt in range(NDT):
                        for ch in range(2):
                            last = (dt == NDT - 1 and ch == 1)
                            P.mm(pg[:, ch * 512:(ch + 1) * 512],
                                 wbuf[k1 % 2][:, dt, ft * 128:(ft + 1) * 128],
                                 A_x1T[:, dt, ch * 512:(ch + 1) * 512],
                                 start=(dt == 0), stop=(dt == NDT - 1),
                                 waits=(w0 if (dt == 0 and ch == 0) else None),
                                 inc=("T" if last else None))
                    g_mile[ft] = P.cnt["T"]
                    silu_mile[ft] = P.act(A_silu[:, ft * 1024:(ft + 1) * 1024],
                                          pg, AF.Silu,
                                          waits={"T": g_mile[ft],
                                                 "V": h_mile_prev.get(ft, 0)}, inc="S")
                sweep_done[k1] = P.cnt["T"]
                emit_wload(k1 + 2)
                for ft in range(8):
                    par = ft % 2
                    pv_ = pbig[:, par * 1024:(par + 1) * 1024]
                    w0 = {"G": wload[k2], "S": silu_mile[7],
                          "V": h_mile.get(ft - 2, 0)}
                    for dt in range(NDT):
                        for ch in range(2):
                            last = (dt == NDT - 1 and ch == 1)
                            P.mm(pv_[:, ch * 512:(ch + 1) * 512],
                                 wbuf[k2 % 2][:, dt, ft * 128:(ft + 1) * 128],
                                 A_x1T[:, dt, ch * 512:(ch + 1) * 512],
                                 start=(dt == 0), stop=(dt == NDT - 1),
                                 waits=(w0 if (dt == 0 and ch == 0) else None),
                                 inc=("T" if last else None))
                    vm = P.cnt["T"]
                    v_mile[ft] = vm
                    h_mile[ft] = P.tt(A_h[:, ft * 1024:(ft + 1) * 1024], pv_,
                                      A_silu[:, ft * 1024:(ft + 1) * 1024], ALU.mult,
                                      waits={"T": vm}, inc="V")
                    if ft == 7:
                        h_mile[7] = vflush(A_h)
                sweep_done[k2] = P.cnt["T"]
                if k2 + 2 <= 47:
                    emit_wload(k2 + 2)
                h_mile_prev = dict(h_mile)
                prev_silu7 = silu_mile[7]
                w3b = _r(bsl(OFF_B + (k3 % 2) * 32768, 16 * 1024), "p (ft d) -> p ft d", d=D)
                for tt in range(8):
                    po = pbig[:, 2048:4096] if tt % 2 == 0 else pbig[:, 0:2048]
                    pob = 2048 if tt % 2 == 0 else 0
                    w0 = {"G": wload[k3], "V": max(h_mile[7], accev.get((e, tt - 2), 0))}
                    for ft in range(8):
                        for ch in range(4):
                            last = (ft == 7 and ch == 3)
                            P.mm(pbig[:, pob + ch * 512: pob + (ch + 1) * 512],
                                 _r(A_h, "p (ft t) -> p ft t", t=1024)[:, ft, tt * 128:(tt + 1) * 128],
                                 w3b[:, ft, ch * 512:(ch + 1) * 512],
                                 start=(ft == 0), stop=(ft == 7),
                                 waits=(w0 if (ft == 0 and ch == 0) else None),
                                 inc=("T" if last else None))
                    w3_m = P.cnt["T"]
                    P.stt(acc[:, tt * 2048:(tt + 1) * 2048],
                          pbig[:, pob: pob + 2048],
                          cgate[:, tt * 8 + e: tt * 8 + e + 1],
                          acc[:, tt * 2048:(tt + 1) * 2048],
                          ALU.mult, ALU.add,
                          waits={"T": w3_m, "Y": acc_ld}, inc="V")
                    accev[(e, tt)] = (vflush(acc[:, tt * 2048:(tt + 1) * 2048])
                                      if e == NE - 1 else P.cnt["V"])
                sweep_done[k3] = P.cnt["T"]
                if k3 + 2 <= 47:
                    emit_wload(k3 + 2)
            for tt in range(8):
                a_t = acc[:, tt * 2048:(tt + 1) * 2048]
                sc = (tt % 2) * 16
                P.act(ln2dump, a_t[:, 0:1024], AF.Identity, accum=stat[:, sc:sc + 1],
                      waits={"V": accev.get((_NE - 1, tt), 0),
                             "Y": acc_ld, "S": prev_silu7})
                P.act(ln2dump, acc[:, tt * 2048 + 1024:(tt + 1) * 2048], AF.Identity,
                      accum=stat[:, sc + 1:sc + 2])
                P.act(ln2dump, a_t[:, 0:1024], AF.Square, accum=stat[:, sc + 2:sc + 3])
                st_m = P.act(ln2dump, acc[:, tt * 2048 + 1024:(tt + 1) * 2048], AF.Square,
                             accum=stat[:, sc + 3:sc + 4], inc="S")
                P.tt(stat[:, sc + 4:sc + 5], stat[:, sc:sc + 1], stat[:, sc + 1:sc + 2],
                     ALU.add, waits={"S": st_m})
                P.tt(stat[:, sc + 5:sc + 6], stat[:, sc + 2:sc + 3], stat[:, sc + 3:sc + 4],
                     ALU.add)
                P.vself()
                P.ts(stat[:, sc + 6:sc + 7], stat[:, sc + 4:sc + 5], 1.0 / D, None, ALU.mult)
                P.ts(stat[:, sc + 7:sc + 8], stat[:, sc + 5:sc + 6], 1.0 / D, None, ALU.mult)
                P.vself()
                P.tt(stat[:, sc + 8:sc + 9], stat[:, sc + 6:sc + 7], stat[:, sc + 6:sc + 7],
                     ALU.mult)
                P.vself()
                P.tt(stat[:, sc + 9:sc + 10], stat[:, sc + 7:sc + 8],
                     stat[:, sc + 8:sc + 9], ALU.subtract)
                var_m = vflush(stat[:, sc + 9:sc + 10])
                P.act(stat[:, sc + 10:sc + 11], stat[:, sc + 9:sc + 10], AF.Sqrt,
                      bias=epsv, waits={"V": var_m}, inc="S")
                sq_m = sflush(stat[:, sc + 10:sc + 11])
                P.recip(stat[:, sc + 11:sc + 12], stat[:, sc + 10:sc + 11], waits={"S": sq_m})
                P.vself()
                P.ts(a_t, a_t, stat[:, sc + 6:sc + 7], stat[:, sc + 11:sc + 12],
                     ALU.subtract, ALU.mult, inc="V")
                n_m = vflush(a_t)
                P.dma("sync", y_out[(hf * 8 + tt) * 128:(hf * 8 + tt + 1) * 128, :], a_t,
                      waits={"V": n_m}, inc="ST")
                y_st_total = P.cnt["ST"]
            p4v_prev = P.cnt["V"]

        if "cg" in taps:
            P.dma("sync", taps["cg"], cgate, waits={"V": P.cnt["V"]}, inc="ST")
        if "hbuf" in taps:
            P.dma("sync", taps["hbuf"], A_h, waits={"V": P.cnt["V"]}, inc="ST")
        _finish()
    return nc


def _host_inputs(L0_attn_x, L0_attn_q_w, L0_attn_kv_w, L0_attn_c_w,
                 L0_ffn_router_w, L0_ffn_w1, L0_ffn_w2, L0_ffn_w3):
    bf = ml_dtypes.bfloat16
    x = np.ascontiguousarray(np.asarray(L0_attn_x, dtype=np.float32))
    q_w = np.asarray(L0_attn_q_w, dtype=np.float32)
    kv_w = np.asarray(L0_attn_kv_w, dtype=np.float32)
    c_w = np.asarray(L0_attn_c_w, dtype=np.float32)
    r_w = np.asarray(L0_ffn_router_w, dtype=np.float32)
    w1 = np.ascontiguousarray(np.asarray(L0_ffn_w1, dtype=np.float32))
    w2 = np.ascontiguousarray(np.asarray(L0_ffn_w2, dtype=np.float32))
    w3 = np.ascontiguousarray(np.asarray(L0_ffn_w3, dtype=np.float32))

    perm = np.concatenate([np.arange(0, 128, 2), np.arange(1, 128, 2)])
    q_perm = q_w.reshape(H, 128, D)[:, perm, :].reshape(D, D)
    k_perm = kv_w[0:512].reshape(G4, 128, D)[:, perm, :].reshape(512, D)
    qwT = np.ascontiguousarray(
        q_perm.T.reshape(16, 128, 16, 128).transpose(2, 1, 0, 3))
    kwT = np.ascontiguousarray(
        k_perm.T.reshape(16, 128, 4, 128).transpose(2, 1, 0, 3))
    vwT = np.ascontiguousarray(
        kv_w[512:1024].T.reshape(16, 128, 512).transpose(1, 0, 2))
    cwT = np.ascontiguousarray(
        c_w.T.reshape(16, 128, D).transpose(1, 0, 2))
    rwT = np.ascontiguousarray(
        r_w.T.reshape(16, 128, 8).transpose(1, 0, 2))
    w1 = np.ascontiguousarray(w1.reshape(NE, 16, 128, DFF).transpose(0, 2, 1, 3))
    w2 = np.ascontiguousarray(w2.reshape(NE, 16, 128, DFF).transpose(0, 2, 1, 3))
    w3 = np.ascontiguousarray(w3.reshape(NE, 8, 128, D).transpose(0, 2, 1, 3))

    iden = np.eye(128, dtype=np.float32).astype(bf)
    ones = np.ones((128, 128), dtype=np.float32).astype(bf)
    kk = np.arange(128)[:, None, None]
    jq = np.arange(4)[None, :, None]
    qq = np.arange(512)[None, None, :]
    masks = (qq >= jq * 128 + kk).astype(np.float32).astype(bf)
    i64 = np.arange(64, dtype=np.float64)
    freqs = 1.0 / (10000.0 ** (2.0 * i64 / DK))
    tpos = np.arange(T, dtype=np.float64)
    ang = tpos[None, :] * freqs[:, None]
    ropeC = np.concatenate([np.cos(ang), np.cos(ang)], axis=0).astype(np.float32)
    ropeS = np.concatenate([-np.sin(ang), np.sin(ang)], axis=0).astype(np.float32)

    shared = {"qwT": qwT, "kwT": kwT, "vwT": vwT, "cwT": cwT, "rwT": rwT,
              "w1": w1, "w2": w2, "w3": w3, "iden": iden, "ones": ones,
              "masks": np.ascontiguousarray(masks), "ropeC": ropeC, "ropeS": ropeS}
    in_maps = []
    for b in range(8):
        m = dict(shared)
        m["x"] = np.ascontiguousarray(x[b])
        m["xT"] = np.ascontiguousarray(
            x[b].T.reshape(16, 128, T).transpose(1, 0, 2))
        in_maps.append(m)
    return in_maps


_tap_results = None
_STAGE = 4


def kernel(**inputs):
    global _tap_results
    in_maps = _host_inputs(**inputs)
    nc = build_program(tuple(_DEBUG), _STAGE)
    res = run_bass_kernel_spmd(nc, in_maps, core_ids=list(range(8)))
    if _DEBUG:
        _tap_results = [{k: v for k, v in r.items() if k.startswith("tap_")}
                        for r in res.results]
    return np.stack([np.asarray(res.results[b]["y"], dtype=np.float32)
                     for b in range(8)], axis=0)


# version 05
# speedup vs baseline: 1.1088x; 1.1088x over previous
import math
import numpy as np
import ml_dtypes
import concourse.bass as bass
import concourse.mybir as mybir
from concourse.bass_utils import run_bass_kernel_spmd

F32 = mybir.dt.float32
BF16 = mybir.dt.bfloat16
U8 = mybir.dt.uint8
AF = mybir.ActivationFunctionType
ALU = mybir.AluOpType
AX = mybir.AxisListType

D = 2048
T = 2048
H = 16
G4 = 4
DK = 128
DFF = 1024
NE = 8
NDT = 16
NTT = 16
EPS = 1e-5
ISQDK = 1.0 / math.sqrt(DK)

_DEBUG = []

OFF_IDEN = 0
OFF_ONES = 256
OFF_MASK = 512
OFF_RWT = 4608
OFF_CGATE = 4864
OFF_STAT = 5120
OFF_EPS = 5376
OFF_DEN = 5440
OFF_DENB = 7488
OFF_RB = 8512
OFF_A = 12672
OFF_B = OFF_A + 65536
OFF_C = OFF_B + 65536
OFF_C2 = OFF_C + 32768
ARENA_BYTES = OFF_C + 65536


class Prog:
    ENGINES = ("tensor", "vector", "scalar", "sync", "gpsimd")
    SEMS = ("T", "V", "S", "G", "Y", "ST")

    def __init__(self):
        self.ops = {e: [] for e in self.ENGINES}
        self.cnt = {s: 0 for s in self.SEMS}

    def add(self, eng, kind, args, waits=None, inc=None, n=1):
        w = {k: v for k, v in (waits or {}).items() if v and v > 0}
        mile = None
        if inc is not None:
            self.cnt[inc] += n
            mile = self.cnt[inc]
        self.ops[eng].append((w, inc, n, kind, args))
        return mile

    def mm(self, out, lhsT, rhs, start, stop, waits=None, inc=None):
        return self.add("tensor", "mm", (out, lhsT, rhs, start, stop), waits, inc)

    def tr(self, out, in_, iden, waits=None, inc=None):
        return self.add("tensor", "tr", (out, in_, iden), waits, inc)

    def tt(self, out, in0, in1, op, waits=None, inc=None):
        return self.add("vector", "tt", (out, in0, in1, op), waits, inc)

    def ts(self, out, in0, s1, s2, op0, op1=None, waits=None, inc=None):
        return self.add("vector", "ts", (out, in0, s1, s2, op0, op1), waits, inc)

    def stt(self, out, in0, sc, in1, op0, op1, waits=None, inc=None):
        return self.add("vector", "stt", (out, in0, sc, in1, op0, op1), waits, inc)

    def red(self, out, in_, op, waits=None, inc=None):
        return self.add("vector", "red", (out, in_, op), waits, inc)

    def vcp(self, out, in_, waits=None, inc=None):
        return self.add("vector", "cp", (out, in_), waits, inc)

    def recip(self, out, in_, waits=None, inc=None):
        return self.add("vector", "recip", (out, in_), waits, inc)

    def vself(self):
        m = self.add("vector", "nopinc", (), None, "V")
        self.add("vector", "wait", (), {"V": m})

    def act(self, out, in_, func, scale=None, bias=None, accum=None, waits=None, inc=None):
        return self.add("scalar", "act", (out, in_, func, scale, bias, accum), waits, inc)

    def scp(self, out, in_, waits=None, inc=None):
        return self.add("scalar", "cp", (out, in_), waits, inc)

    def dma(self, q, out, in_, waits=None, inc=None):
        return self.add(q, "dma", (out, in_), waits, inc, n=16)

    def mset(self, ap, val, waits=None, inc=None):
        return self.add("vector", "mset", (ap, val), waits, inc)

    def fin(self, eng, waits):
        self.add(eng, "wait", (), waits)

    def replay(self, eng, name, sems):
        for (w, inc, n, kind, a) in self.ops[name]:
            for s, v in w.items():
                eng.wait_ge(sems[s], v)
            r = None
            if kind == "mm":
                r = eng.matmul(a[0], a[1], a[2], start=a[3], stop=a[4])
            elif kind == "tr":
                r = eng.transpose(a[0], a[1], a[2])
            elif kind == "tt":
                r = eng.tensor_tensor(a[0], a[1], a[2], op=a[3])
            elif kind == "ts":
                r = eng.tensor_scalar(a[0], a[1], a[2], a[3], op0=a[4],
                                      op1=(a[5] if a[5] is not None else ALU.bypass))
            elif kind == "stt":
                r = eng.scalar_tensor_tensor(a[0], a[1], a[2], a[3], op0=a[4], op1=a[5])
            elif kind == "red":
                r = eng.tensor_reduce(a[0], a[1], axis=AX.X, op=a[2])
            elif kind == "cp":
                r = eng.tensor_copy(a[0], a[1]) if name == "vector" else eng.copy(a[0], a[1])
            elif kind == "recip":
                r = eng.reciprocal(a[0], a[1])
            elif kind == "act":
                kwargs = {}
                if a[3] is not None:
                    kwargs["scale"] = a[3]
                if a[4] is not None:
                    kwargs["bias"] = a[4]
                if a[5] is not None:
                    kwargs["accum_out"] = a[5]
                r = eng.activation(a[0], a[1], a[2], **kwargs)
            elif kind == "dma":
                r = eng.dma_start(out=a[0], in_=a[1])
            elif kind == "mset":
                r = eng.memset(a[0], a[1])
            elif kind == "nopinc":
                r = None
            elif kind == "wait":
                continue
            if inc is not None:
                if r is not None:
                    r.then_inc(sems[inc], n)
                else:
                    eng.sem_inc(sems[inc], n)


def _r(ap, pattern, **kw):
    return ap.rearrange(pattern, **kw)


def build_program(debug_taps=(), stage=4):
    nc = bass.Bass("TRN2", target_bir_lowering=False)

    x_in = nc.dram_tensor("x", [T, D], F32, kind="ExternalInput").ap()
    xT_in = nc.dram_tensor("xT", [128, NDT, T], F32, kind="ExternalInput").ap()
    qwT_in = nc.dram_tensor("qwT", [16, 128, NDT, 128], F32, kind="ExternalInput").ap()
    kwT_in = nc.dram_tensor("kwT", [4, 128, NDT, 128], F32, kind="ExternalInput").ap()
    vwT_in = nc.dram_tensor("vwT", [128, NDT, 512], F32, kind="ExternalInput").ap()
    cwT_in = nc.dram_tensor("cwT", [128, NDT, D], F32, kind="ExternalInput").ap()
    rwT_in = nc.dram_tensor("rwT", [128, NDT, 8], F32, kind="ExternalInput").ap()
    w1_in = nc.dram_tensor("w1", [NE, 128, NDT, DFF], F32, kind="ExternalInput").ap()
    w2_in = nc.dram_tensor("w2", [NE, 128, NDT, DFF], F32, kind="ExternalInput").ap()
    w3_in = nc.dram_tensor("w3", [NE, 128, 8, D], F32, kind="ExternalInput").ap()
    iden_in = nc.dram_tensor("iden", [128, 128], BF16, kind="ExternalInput").ap()
    ones_in = nc.dram_tensor("ones", [128, 128], BF16, kind="ExternalInput").ap()
    mask_in = nc.dram_tensor("masks", [128, 4, 512], BF16, kind="ExternalInput").ap()
    ropeC_in = nc.dram_tensor("ropeC", [128, T], F32, kind="ExternalInput").ap()
    ropeS_in = nc.dram_tensor("ropeS", [128, T], F32, kind="ExternalInput").ap()
    y_out = nc.dram_tensor("y", [T, D], F32, kind="ExternalOutput").ap()
    x1f_d = nc.dram_tensor("x1f", [NTT, 128, D], F32, kind="Internal").ap()
    taps = {}
    for t_ in debug_taps:
        shp = {"qt": [128, H * T], "kt": [128, G4 * T], "v": [128, NTT * 512],
               "ot": [128, H * T], "x1": [T, D], "expt": [128, 16 * 512],
               "den": [128, 512], "rbs": [128, 512], "cg": [128, 64],
               "hbuf": [128, 8 * 1024]}[t_]
        dt_ = F32 if t_ in ("x1", "den", "rbs", "cg") else BF16
        taps[t_] = nc.dram_tensor("tap_" + t_, shp, dt_, kind="ExternalOutput").ap()

    from contextlib import ExitStack
    with ExitStack() as ctx:
        en = ctx.enter_context
        arena = en(nc.sbuf_tensor("arena", [128, ARENA_BYTES], U8))
        af = arena.bitcast(F32)
        ab = arena.bitcast(BF16)
        pbig = en(nc.psum_tensor("pbig", [128, 4096], F32))
        pbig_bf = pbig.bitcast(BF16)

        def fsl(off, n, p=slice(None)):
            return af[p, off // 4: off // 4 + n]

        def bsl(off, n, p=slice(None)):
            return ab[p, off // 2: off // 2 + n]

        iden = bsl(OFF_IDEN, 128)
        onesb = bsl(OFF_ONES, 128)
        scrS = fsl(5392, 2, p=slice(0, 1))
        scrV = fsl(5408, 2, p=slice(0, 1))
        masks = bsl(OFF_MASK, 4 * 512)
        rwT = bsl(OFF_RWT, 16 * 8)
        rwT3 = _r(rwT, "p (dt e) -> p dt e", e=8)
        cgate = fsl(OFF_CGATE, 64)
        stat = fsl(OFF_STAT, 64)
        epsv = fsl(OFF_EPS, 1)
        sden = fsl(OFF_DEN, 512, p=slice(0, 1))
        sdenb = bsl(OFF_DENB, 512, p=slice(0, 1))
        rb = [fsl(OFF_RB, 512), fsl(OFF_RB + 2048, 512)]
        A_bf = bsl(OFF_A, 32768)
        A_xT = _r(A_bf, "p (dt t) -> p dt t", t=T)
        A_cwT = _r(A_bf, "p (dt o) -> p dt o", o=D)
        A_x1T = _r(bsl(OFF_A, 16 * 1024), "p (dt t) -> p dt t", t=1024)
        A_silu = bsl(OFF_A + 32768, 8 * 1024)
        A_h = bsl(OFF_A + 49152, 8 * 1024)
        B_bf = bsl(OFF_B, 32768)
        wbuf = [_r(bsl(OFF_B, 16 * 1024), "p (dt f) -> p dt f", f=1024),
                _r(bsl(OFF_B + 32768, 16 * 1024), "p (dt f) -> p dt f", f=1024)]
        KT = bsl(OFF_C, 4 * T)
        Vt = bsl(OFF_C + 16384, 16 * 512)
        ropeC = fsl(OFF_C2, T)
        ropeS = fsl(OFF_C2 + 8192, T)
        tmp1 = fsl(OFF_C2 + 16384, 1024)
        tmp2 = fsl(OFF_C2 + 20480, 1024)
        wstream = [bsl(OFF_C2 + 24576, 2048), bsl(OFF_C2 + 28672, 2048)]
        vwTb = _r(bsl(OFF_C2 + 16384, 16 * 512), "p (dt o) -> p dt o", o=512)
        if globals().get("_DBG_RING", False):
            expT = bsl(OFF_A + 32768, 16 * 512)
        else:
            expT = bsl(OFF_C2, 16 * 512)
        xres = [fsl(OFF_C2 + 16384, 2048), fsl(OFF_C2 + 24576, 2048)]
        sscr2 = [fsl(OFF_C, 2048), fsl(OFF_C + 8192, 2048)]
        lndump = fsl(OFF_C + 16384, 2048)
        x1b = bsl(OFF_C2 + 16384, 2048)
        x1stage = [bsl(OFF_C2 + 20480, 2048), bsl(OFF_C2 + 24576, 2048)]
        acc = fsl(OFF_C, 8 * 2048)
        ln2dump = fsl(OFF_MASK, 1024)

        P = Prog()

        def vflush(src):
            return P.vcp(scrV[:, 0:1], src[0:1, 0:1], inc="V")

        def sflush(src):
            return P.scp(scrS[:, 0:1], src[0:1, 0:1], inc="S")

        def _finish():
            P.fin("sync", {"ST": P.cnt["ST"]})
            sems = {}
            for s in Prog.SEMS:
                sems[s] = en(nc.semaphore("sem" + s))
            block = en(nc.Block())

            @block.tensor
            def _(eng):
                P.replay(eng, "tensor", sems)

            @block.vector
            def _(eng):
                P.replay(eng, "vector", sems)

            @block.scalar
            def _(eng):
                P.replay(eng, "scalar", sems)

            @block.sync
            def _(eng):
                P.replay(eng, "sync", sems)

            @block.gpsimd
            def _(eng):
                P.replay(eng, "gpsimd", sems)

        P.dma("sync", iden, iden_in, inc="Y")
        P.dma("sync", onesb, ones_in, inc="Y")
        P.dma("sync", _r(masks, "p (a b) -> p a b", a=4), mask_in, inc="Y")
        P.dma("sync", ropeC, ropeC_in, inc="Y")
        yconst = P.dma("sync", ropeS, ropeS_in, inc="Y")
        P.mset(epsv, EPS, inc="V")

        g_rwT = P.dma("gpsimd", rwT3, rwT_in, inc="G")
        g_w0 = P.dma("gpsimd",
                     _r(wstream[0], "p (dt o) -> p dt o", o=128),
                     _r(qwT_in[0:1], "a p dt o -> p (a dt) o"), inc="G")
        g_xT = P.dma("gpsimd", A_xT, xT_in, inc="G")
        P.dma("gpsimd", bsl(OFF_B, 2048), xT_in[:, 0, 0:2048], inc="G")
        g_canary = P.dma("gpsimd", fsl(OFF_DEN, 16), xT_in[:, 0, 0:16], inc="G")

        NQO = 20
        g_wtile = {0: g_w0}
        proj_done = {}
        rope_evac = {}
        for ot in range(NQO):
            if ot > 0:
                src = qwT_in[ot:ot + 1] if ot < 16 else kwT_in[ot - 16:ot - 15]
                w = {"T": proj_done.get(ot - 2, 0)} if ot >= 2 else None
                g_wtile[ot] = P.dma("gpsimd",
                                    _r(wstream[ot % 2], "p (dt o) -> p dt o", o=128),
                                    _r(src, "a p dt o -> p (a dt) o"), waits=w, inc="G")
            par = ot % 2
            pq = pbig[:, par * 2048:(par + 1) * 2048]
            buf3 = _r(wstream[ot % 2], "p (dt o) -> p dt o", o=128)
            gw = max(g_xT, g_wtile[ot])
            if ot == 0 and g_canary:
                gw = max(gw, g_canary)
            w0 = {"G": gw, "V": rope_evac.get(ot - 2, 0)}
            for dt in range(NDT):
                for nch in range(4):
                    last = (dt == NDT - 1 and nch == 3)
                    P.mm(pq[:, nch * 512:(nch + 1) * 512], buf3[:, dt, :],
                         A_xT[:, dt, nch * 512:(nch + 1) * 512],
                         start=(dt == 0), stop=(dt == NDT - 1),
                         waits=(w0 if (dt == 0 and nch == 0) else None),
                         inc=("T" if last else None))
            proj_done[ot] = P.cnt["T"]
            dst = bsl(OFF_B + ot * 4096, T) if ot < 16 else bsl(OFF_C + (ot - 16) * 4096, T)
            wv = {"T": proj_done[ot], "Y": yconst}
            for ch in range(2):
                cs = slice(ch * 1024, (ch + 1) * 1024)
                P.tt(tmp1, pq[:, cs], ropeC[:, cs], ALU.mult, waits=(wv if ch == 0 else None))
                P.tt(tmp2[0:64, :], pq[64:128, cs], ropeS[0:64, cs], ALU.mult)
                P.tt(tmp2[64:128, :], pq[0:64, cs], ropeS[64:128, cs], ALU.mult)
                P.vself()
                P.tt(dst[:, cs], tmp1, tmp2, ALU.add, inc=("V" if ch == 1 else None))
            rope_evac[ot] = P.cnt["V"]
        rope_all = P.cnt["V"]

        P.dma("gpsimd", vwTb, vwT_in, waits={"V": rope_evac[19]}, inc="G")
        P.dma("gpsimd", bsl(OFF_DENB, 512), vwT_in[:, 0, 0:512], inc="G")
        g_vwT = P.dma("gpsimd", fsl(OFF_DEN, 16), vwT_in[:, 0, 0:16], inc="G")
        vevac = {}
        for tt in range(NTT):
            pv = pbig[:, (tt % 2) * 2048:(tt % 2) * 2048 + 512]
            w0 = {"G": g_vwT, "V": rope_all, "S": vevac.get(tt - 2, 0)}
            for dt in range(NDT):
                P.mm(pv, A_xT[:, dt, tt * 128:(tt + 1) * 128], vwTb[:, dt, :],
                     start=(dt == 0), stop=(dt == NDT - 1),
                     waits=(w0 if dt == 0 else None),
                     inc=("T" if dt == NDT - 1 else None))
            m = P.cnt["T"]
            vevac[tt] = P.scp(Vt[:, tt * 512:(tt + 1) * 512], pv, waits={"T": m}, inc="S")
        p1_T = P.cnt["T"]
        p1_S = P.cnt["S"]

        if "qt" in taps:
            P.dma("sync", taps["qt"], B_bf, waits={"V": rope_all}, inc="ST")
        if "kt" in taps:
            P.dma("sync", taps["kt"], KT, waits={"V": rope_all}, inc="ST")
        if "v" in taps:
            P.dma("sync", taps["v"], Vt, waits={"S": p1_S}, inc="ST")

        if stage < 2:
            _finish()
            return nc

        g_cwT = P.dma("gpsimd", A_cwT, cwT_in, waits={"T": p1_T}, inc="G")

        prev_evac = [0, 0]
        prev_rbm = [0, 0]
        lastexp = [0, 0, 0, 0]
        exp_mile = {}
        pend = {}

        def flush_tail():
            if not pend:
                return
            par, h_, qs_, pv_stop, exp_last, recip_m = (
                pend["par"], pend["h"], pend["qs"],
                pend["pv_stop"], pend["exp_last"], pend["recip_m"])
            dbank = pbig[:, (6 + par) * 512: (7 + par) * 512]
            obank = pbig[:, (4 + par) * 512: (5 + par) * 512]
            bc_m = P.mm(dbank, onesb[0:1, :], sdenb, True, True,
                        waits={"V": recip_m, "S": exp_last}, inc="T")
            P.scp(rb[par], dbank, waits={"T": bc_m, "V": prev_evac[par]}, inc="S")
            prev_rbm[par] = sflush(rb[par])
            prev_evac[par] = P.tt(bsl(OFF_B + h_ * 4096, T)[:, qs_], obank, rb[par],
                                  ALU.mult, waits={"S": prev_rbm[par], "T": pv_stop},
                                  inc="V")
            pend.clear()

        _H2 = globals().get("_P2H", H)
        _C2n = globals().get("_P2C", 4)
        _horder = globals().get("_HORDER", None) or list(range(_H2))
        first_q = True
        for h in _horder:
            g = h // 4
            QTh = bsl(OFF_B + h * 4096, T)
            for c in range(_C2n):
                par = (h * 4 + c) % 2
                jmax = 4 * c + 3
                qs = slice(c * 512, (c + 1) * 512)
                lt_mile = {}
                mask_last = 0
                lt_bank = lambda j: pbig[:, (j % 4) * 512: (j % 4) * 512 + 512]
                esl = lambda j: expT[:, j * 512:(j + 1) * 512]
                obank = pbig[:, (4 + par) * 512: (5 + par) * 512]
                dbank = pbig[0:1, (6 + par) * 512: (7 + par) * 512]

                def lt(j):
                    w = {"S": lastexp[j % 4]}
                    if first_q and j == 0:
                        w["V"] = rope_all
                        w["S"] = max(w["S"], p1_S)
                    lt_mile[j] = P.mm(lt_bank(j),
                                      KT[:, g * T + j * 128: g * T + (j + 1) * 128],
                                      QTh[:, qs], True, True, waits=w, inc="T")

                def den_pv(j, dwait):
                    if j == 0:
                        dwait = dict(dwait)
                        dwait["S"] = max(dwait.get("S", 0), prev_rbm[par])
                        dwait["V"] = max(dwait.get("V", 0), prev_evac[par])
                    P.mm(dbank, onesb[:, 0:1], esl(j), j == 0, j == jmax,
                         waits=dwait, inc=("T" if j == jmax else None))
                    P.mm(obank, Vt[:, j * 512 + g * 128: j * 512 + g * 128 + 128],
                         esl(j), j == 0, j == jmax,
                         inc=("T" if j == jmax else None))

                lt(0)
                first_q = False
                mask_m = {}
                for j in range(jmax + 1):
                    if j + 1 <= jmax:
                        lt(j + 1)
                    exp_mile[(h, c, j)] = P.act(esl(j), lt_bank(j),
                                                AF.Exp, scale=ISQDK,
                                                waits={"T": lt_mile[j]}, inc="S")
                    lastexp[j % 4] = exp_mile[(h, c, j)]
                    if j >= 4 * c:
                        mask_last = P.tt(esl(j), esl(j),
                                         masks[:, (j - 4 * c) * 512:(j - 4 * c + 1) * 512],
                                         ALU.mult, waits={"S": exp_mile[(h, c, j)]}, inc="V")
                    mask_m[j] = mask_last
                    if j >= 2:
                        den_pv(j - 2, {"S": exp_mile[(h, c, j)], "V": mask_m[j - 2]})
                    if j == 3:
                        flush_tail()
                sf = sflush(esl(jmax))
                vf = vflush(esl(jmax))
                den_pv(jmax - 1, {"S": sf, "V": vf})
                den_pv(jmax, {"S": sf, "V": vf})
                pv_stop = P.cnt["T"]
                den_stop = pv_stop - 1
                P.vcp(sden, dbank, waits={"T": den_stop}, inc="V")
                P.add("vector", "wait", (), {"V": P.cnt["V"]})
                P.recip(sden, sden, inc="V")
                P.add("vector", "wait", (), {"V": P.cnt["V"]})
                P.vcp(sdenb, sden, inc="V")
                P.add("vector", "wait", (), {"V": P.cnt["V"]})
                pend.update(par=par, h=h, qs=qs, pv_stop=pv_stop,
                            exp_last=exp_mile[(h, c, jmax)], recip_m=vflush(sdenb))
        flush_tail()
        p2_T = P.cnt["T"]
        vflush(bsl(OFF_B + (H - 1) * 4096, T))
        p2_V = P.cnt["V"]


        if "ot" in taps:
            P.dma("sync", taps["ot"], B_bf, waits={"V": p2_V}, inc="ST")
        if "expt" in taps:
            P.dma("sync", taps["expt"], expT, waits={"V": p2_V, "S": P.cnt["S"]}, inc="ST")
        if "den" in taps:
            P.dma("sync", taps["den"], fsl(OFF_DEN, 512), waits={"V": p2_V}, inc="ST")
        if "rbs" in taps:
            P.dma("sync", taps["rbs"], rb[0], waits={"V": p2_V, "S": P.cnt["S"]}, inc="ST")
        if stage < 3:
            _finish()
            return nc

        xres_mile = {}
        resid_add = {}
        norm_mile = {}
        x1b_mile = {}
        ttev = {}
        tr_done = {}
        x1f_store = {}
        stage_store = {}
        for tt in range(NTT):
            s3 = tt % 2
            base = s3 * 2048
            w = {"T": p1_T} if tt < 2 else {"V": resid_add[tt - 2]}
            xres_mile[tt] = P.dma("sync", xres[tt % 2], x_in[tt * 128:(tt + 1) * 128, :],
                                  waits=w, inc="Y")
            w0 = {"G": g_cwT, "V": p2_V}
            if tt >= 2:
                w0["V"] = max(w0["V"], resid_add[tt - 2])
            for dt in range(NDT):
                for nch in range(4):
                    last = (dt == NDT - 1 and nch == 3)
                    P.mm(pbig[:, base + nch * 512: base + (nch + 1) * 512],
                         bsl(OFF_B + dt * 4096, T)[:, tt * 128:(tt + 1) * 128],
                         A_cwT[:, dt, nch * 512:(nch + 1) * 512],
                         start=(dt == 0), stop=(dt == NDT - 1),
                         waits=(w0 if (dt == 0 and nch == 0) else None),
                         inc=("T" if last else None))
            cp_m = P.cnt["T"]
            sscr = sscr2[tt % 2]
            P.tt(sscr, pbig[:, base: base + 2048], xres[tt % 2], ALU.add,
                 waits={"T": cp_m, "Y": max(xres_mile[tt],
                                            x1f_store.get(tt - 2, 0))},
                 inc="V")
            resid_add[tt] = P.cnt["V"]
            ra_f = vflush(sscr)
            sc = (tt % 4) * 16
            P.act(lndump, sscr, AF.Identity, accum=stat[:, sc:sc + 1],
                  waits={"V": ra_f})
            st_m = P.act(lndump, sscr, AF.Square, accum=stat[:, sc + 1:sc + 2], inc="S")
            st_m = sflush(stat[:, sc:sc + 2])
            P.ts(stat[:, sc + 2:sc + 3], stat[:, sc:sc + 1], 1.0 / D, None, ALU.mult,
                 waits={"S": st_m})
            P.ts(stat[:, sc + 3:sc + 4], stat[:, sc + 1:sc + 2], 1.0 / D, None, ALU.mult)
            P.vself()
            P.tt(stat[:, sc + 4:sc + 5], stat[:, sc + 2:sc + 3], stat[:, sc + 2:sc + 3], ALU.mult)
            P.vself()
            P.tt(stat[:, sc + 5:sc + 6], stat[:, sc + 3:sc + 4], stat[:, sc + 4:sc + 5],
                 ALU.subtract)
            var_m = vflush(stat[:, sc + 5:sc + 6])
            P.act(stat[:, sc + 6:sc + 7], stat[:, sc + 5:sc + 6], AF.Sqrt, bias=epsv,
                  waits={"V": var_m}, inc="S")
            sq_m = sflush(stat[:, sc + 6:sc + 7])
            P.recip(stat[:, sc + 7:sc + 8], stat[:, sc + 6:sc + 7], waits={"S": sq_m})
            P.vself()
            norm_mile[tt] = P.ts(sscr, sscr, stat[:, sc + 2:sc + 3], stat[:, sc + 7:sc + 8],
                                 ALU.subtract, ALU.mult, inc="V")
            nf = vflush(sscr)
            x1f_store[tt] = P.dma("sync", x1f_d[tt], sscr, waits={"V": nf}, inc="Y")
            if "x1" in taps:
                P.dma("sync", taps["x1"][tt * 128:(tt + 1) * 128, :], sscr, inc="ST")
        p3_T = P.cnt["T"]
        p3_V = P.cnt["V"]
        p3_S = P.cnt["S"]
        if stage < 4:
            _finish()
            return nc

        wload = {}
        sweep_done = {}
        h_mile_prev = {}
        prev_silu7 = 0
        p4v_prev = 0
        y_st_total = 0
        for hf in range(2):
            wacc = {"V": p3_V, "S": p3_S, "T": p3_T} if hf == 0 else \
                {"V": p4v_prev, "ST": y_st_total, "T": sweep_done.get(hf * 24 - 1, 0)}
            P.dma("sync", _r(acc, "p (tt d) -> p tt d", d=D),
                  _r(x1f_d[hf * 8:(hf + 1) * 8], "tt p d -> p tt d"),
                  waits=wacc, inc="Y")
            P.dma("sync", fsl(OFF_DEN, 512), x1f_d[0][0:128, 0:512], inc="Y")
            acc_ld = P.dma("sync", fsl(OFF_DEN, 16), x1f_d[0][0:128, 0:16], inc="Y")
            x1b4 = bsl(OFF_A + 32768, 2048)
            trev = {}
            tr_t = {}
            for tt in range(8):
                s4 = tt % 2
                wv = {"Y": acc_ld}
                if tt >= 1:
                    wv["T"] = tr_t[tt - 1]
                P.vcp(x1b4, acc[:, tt * 2048:(tt + 1) * 2048], waits=wv, inc="V")
                x1b4_m = vflush(x1b4)
                for dt in range(NDT):
                    bko = (s4 * 2048 + (dt // 4) * 512) * 2 + (dt % 4) * 128
                    w = None
                    if dt == 0:
                        w = {"V": x1b4_m}
                        if tt >= 2:
                            w["V"] = max(w["V"], trev[(tt - 2, 3)])
                    P.tr(pbig_bf[:, bko: bko + 128], x1b4[:, dt * 128:(dt + 1) * 128],
                         iden, waits=w, inc=("T" if dt % 4 == 3 else None))
                    if dt % 4 == 3:
                        grp = dt // 4
                        trev[(tt, grp)] = P.vcp(
                            A_x1T[:, grp * 4:(grp + 1) * 4, tt * 128:(tt + 1) * 128],
                            _r(pbig_bf[:, (s4 * 2048 + grp * 512) * 2:
                                       (s4 * 2048 + grp * 512) * 2 + 512],
                               "p (a b) -> p a b", b=128),
                            waits={"T": P.cnt["T"]}, inc="V")
                tr_t[tt] = P.cnt["T"]
            x1t_gate = vflush(bsl(OFF_A, 4))
            x1t_ld = acc_ld
            def emit_wload(k):
                e_, m_ = (k % 24) // 3, (k % 24) % 3
                src = (w1_in, w2_in, w3_in)[m_]
                pat = "a p dt f -> p (a dt) f" if m_ < 2 else "a p ft d -> p (a ft) d"
                s_ap = _r(src[e_:e_ + 1], pat)
                w = {"T": sweep_done.get(k - 2, p3_T if k < 2 else 0)}
                dst = wbuf[k % 2] if m_ < 2 else \
                    _r(bsl(OFF_B + (k % 2) * 32768, 16 * 1024), "p (ft d) -> p ft d", d=D)
                wload[k] = P.dma("gpsimd", dst, s_ap, waits=w, inc="G")
                if k % 24 < 3:
                    P.dma("gpsimd", bsl(OFF_DENB, 512), xT_in[:, 0, 0:512], inc="G")
                    wload[k] = P.dma("gpsimd", fsl(OFF_DEN, 16),
                                     xT_in[:, 0, 0:16], inc="G")
                return wload[k]

            emit_wload(hf * 24 + 0)
            emit_wload(hf * 24 + 1)
            lgcp = {}
            cg_mile = {}
            lg = stat[:, 16:24]
            prt = pbig[:, 2048:2056]
            for ht in range(8):
                w0 = {"Y": x1t_ld, "G": g_rwT,
                      "V": max(lgcp.get(ht - 1, p4v_prev), x1t_gate)}
                for dt in range(NDT):
                    P.mm(prt, A_x1T[:, dt, ht * 128:(ht + 1) * 128], rwT3[:, dt, :],
                         start=(dt == 0), stop=(dt == NDT - 1),
                         waits=(w0 if dt == 0 else None),
                         inc=("T" if dt == NDT - 1 else None))
                rt_m = P.cnt["T"]
                lgcp[ht] = P.vcp(lg, prt, waits={"T": rt_m}, inc="V")
                P.vself()
                P.red(stat[:, 24:25], lg, ALU.max)
                P.vself()
                P.ts(stat[:, 32:40], lg, stat[:, 24:25], None, ALU.is_ge)
                P.vself()
                P.stt(stat[:, 40:48], stat[:, 32:40], -1e30, lg, ALU.mult, ALU.add)
                P.vself()
                P.red(stat[:, 25:26], stat[:, 40:48], ALU.max)
                P.vself()
                P.ts(stat[:, 48:56], stat[:, 40:48], stat[:, 25:26], None, ALU.is_ge)
                P.tt(stat[:, 26:27], stat[:, 24:25], stat[:, 25:26], ALU.subtract)
                d12_m = vflush(stat[:, 26:27])
                P.act(stat[:, 27:28], stat[:, 26:27], AF.Exp, scale=-1.0,
                      waits={"V": d12_m}, inc="S")
                e12_m = sflush(stat[:, 27:28])
                P.ts(stat[:, 28:29], stat[:, 27:28], 1.0, None, ALU.add, waits={"S": e12_m})
                P.vself()
                P.recip(stat[:, 29:30], stat[:, 28:29])
                P.vself()
                P.ts(stat[:, 30:31], stat[:, 29:30], -1.0, 1.0, ALU.mult, ALU.add)
                P.vself()
                P.ts(stat[:, 56:64], stat[:, 48:56], stat[:, 30:31], None, ALU.mult)
                cg_mile[ht] = P.stt(cgate[:, ht * 8:(ht + 1) * 8], stat[:, 32:40],
                                    stat[:, 29:30], stat[:, 56:64], ALU.mult, ALU.add,
                                    inc="V")
            accev = {}
            _NE = globals().get("_P4E", NE)
            for e in range(_NE):
                k1, k2, k3 = hf * 24 + 3 * e, hf * 24 + 3 * e + 1, hf * 24 + 3 * e + 2
                silu_mile = {}
                v_mile = {}
                h_mile = {}
                g_mile = {}
                for ft in range(8):
                    par = ft % 2
                    pg = pbig[:, par * 1024:(par + 1) * 1024]
                    sgate = silu_mile.get(ft - 2, 0)
                    if ft < 2:
                        sgate = max(sgate, prev_silu7)
                    w0 = {"G": wload[k1], "S": sgate,
                          "V": max(cg_mile[7], x1t_gate)}
                    if e > 0:
                        w0["V"] = max(w0["V"], accev.get((e - 1, 7), 0))
                    for dt in range(NDT):
                        for ch in range(2):
                            last = (dt == NDT - 1 and ch == 1)
                            P.mm(pg[:, ch * 512:(ch + 1) * 512],
                                 wbuf[k1 % 2][:, dt, ft * 128:(ft + 1) * 128],
                                 A_x1T[:, dt, ch * 512:(ch + 1) * 512],
                                 start=(dt == 0), stop=(dt == NDT - 1),
                                 waits=(w0 if (dt == 0 and ch == 0) else None),
                                 inc=("T" if last else None))
                    g_mile[ft] = P.cnt["T"]
                    silu_mile[ft] = P.act(A_silu[:, ft * 1024:(ft + 1) * 1024],
                                          pg, AF.Silu,
                                          waits={"T": g_mile[ft],
                                                 "V": h_mile_prev.get(ft, 0)}, inc="S")
                sweep_done[k1] = P.cnt["T"]
                emit_wload(k1 + 2)
                for ft in range(8):
                    par = ft % 2
                    pv_ = pbig[:, par * 1024:(par + 1) * 1024]
                    w0 = {"G": wload[k2], "S": silu_mile[7],
                          "V": h_mile.get(ft - 2, 0)}
                    for dt in range(NDT):
                        for ch in range(2):
                            last = (dt == NDT - 1 and ch == 1)
                            P.mm(pv_[:, ch * 512:(ch + 1) * 512],
                                 wbuf[k2 % 2][:, dt, ft * 128:(ft + 1) * 128],
                                 A_x1T[:, dt, ch * 512:(ch + 1) * 512],
                                 start=(dt == 0), stop=(dt == NDT - 1),
                                 waits=(w0 if (dt == 0 and ch == 0) else None),
                                 inc=("T" if last else None))
                    vm = P.cnt["T"]
                    v_mile[ft] = vm
                    h_mile[ft] = P.tt(A_h[:, ft * 1024:(ft + 1) * 1024], pv_,
                                      A_silu[:, ft * 1024:(ft + 1) * 1024], ALU.mult,
                                      waits={"T": vm}, inc="V")
                    if ft == 7:
                        h_mile[7] = vflush(A_h)
                sweep_done[k2] = P.cnt["T"]
                if k2 + 2 <= 47:
                    emit_wload(k2 + 2)
                h_mile_prev = dict(h_mile)
                prev_silu7 = silu_mile[7]
                w3b = _r(bsl(OFF_B + (k3 % 2) * 32768, 16 * 1024), "p (ft d) -> p ft d", d=D)
                for tt in range(8):
                    po = pbig[:, 2048:4096] if tt % 2 == 0 else pbig[:, 0:2048]
                    pob = 2048 if tt % 2 == 0 else 0
                    w0 = {"G": wload[k3], "V": max(h_mile[7], accev.get((e, tt - 2), 0))}
                    for ft in range(8):
                        for ch in range(4):
                            last = (ft == 7 and ch == 3)
                            P.mm(pbig[:, pob + ch * 512: pob + (ch + 1) * 512],
                                 _r(A_h, "p (ft t) -> p ft t", t=1024)[:, ft, tt * 128:(tt + 1) * 128],
                                 w3b[:, ft, ch * 512:(ch + 1) * 512],
                                 start=(ft == 0), stop=(ft == 7),
                                 waits=(w0 if (ft == 0 and ch == 0) else None),
                                 inc=("T" if last else None))
                    w3_m = P.cnt["T"]
                    P.stt(acc[:, tt * 2048:(tt + 1) * 2048],
                          pbig[:, pob: pob + 2048],
                          cgate[:, tt * 8 + e: tt * 8 + e + 1],
                          acc[:, tt * 2048:(tt + 1) * 2048],
                          ALU.mult, ALU.add,
                          waits={"T": w3_m, "Y": acc_ld}, inc="V")
                    accev[(e, tt)] = (vflush(acc[:, tt * 2048:(tt + 1) * 2048])
                                      if e == NE - 1 else P.cnt["V"])
                sweep_done[k3] = P.cnt["T"]
                if k3 + 2 <= 47:
                    emit_wload(k3 + 2)
            for tt in range(8):
                a_t = acc[:, tt * 2048:(tt + 1) * 2048]
                sc = (tt % 2) * 16
                P.act(ln2dump, a_t[:, 0:1024], AF.Identity, accum=stat[:, sc:sc + 1],
                      waits={"V": accev.get((_NE - 1, tt), 0),
                             "Y": acc_ld, "S": prev_silu7})
                P.act(ln2dump, acc[:, tt * 2048 + 1024:(tt + 1) * 2048], AF.Identity,
                      accum=stat[:, sc + 1:sc + 2])
                P.act(ln2dump, a_t[:, 0:1024], AF.Square, accum=stat[:, sc + 2:sc + 3])
                st_m = P.act(ln2dump, acc[:, tt * 2048 + 1024:(tt + 1) * 2048], AF.Square,
                             accum=stat[:, sc + 3:sc + 4], inc="S")
                P.tt(stat[:, sc + 4:sc + 5], stat[:, sc:sc + 1], stat[:, sc + 1:sc + 2],
                     ALU.add, waits={"S": st_m})
                P.tt(stat[:, sc + 5:sc + 6], stat[:, sc + 2:sc + 3], stat[:, sc + 3:sc + 4],
                     ALU.add)
                P.vself()
                P.ts(stat[:, sc + 6:sc + 7], stat[:, sc + 4:sc + 5], 1.0 / D, None, ALU.mult)
                P.ts(stat[:, sc + 7:sc + 8], stat[:, sc + 5:sc + 6], 1.0 / D, None, ALU.mult)
                P.vself()
                P.tt(stat[:, sc + 8:sc + 9], stat[:, sc + 6:sc + 7], stat[:, sc + 6:sc + 7],
                     ALU.mult)
                P.vself()
                P.tt(stat[:, sc + 9:sc + 10], stat[:, sc + 7:sc + 8],
                     stat[:, sc + 8:sc + 9], ALU.subtract)
                var_m = vflush(stat[:, sc + 9:sc + 10])
                P.act(stat[:, sc + 10:sc + 11], stat[:, sc + 9:sc + 10], AF.Sqrt,
                      bias=epsv, waits={"V": var_m}, inc="S")
                sq_m = sflush(stat[:, sc + 10:sc + 11])
                P.recip(stat[:, sc + 11:sc + 12], stat[:, sc + 10:sc + 11], waits={"S": sq_m})
                P.vself()
                P.ts(a_t, a_t, stat[:, sc + 6:sc + 7], stat[:, sc + 11:sc + 12],
                     ALU.subtract, ALU.mult, inc="V")
                n_m = vflush(a_t)
                P.dma("sync", y_out[(hf * 8 + tt) * 128:(hf * 8 + tt + 1) * 128, :], a_t,
                      waits={"V": n_m}, inc="ST")
                y_st_total = P.cnt["ST"]
            p4v_prev = P.cnt["V"]

        if "cg" in taps:
            P.dma("sync", taps["cg"], cgate, waits={"V": P.cnt["V"]}, inc="ST")
        if "hbuf" in taps:
            P.dma("sync", taps["hbuf"], A_h, waits={"V": P.cnt["V"]}, inc="ST")
        _finish()
    return nc


def _host_inputs(L0_attn_x, L0_attn_q_w, L0_attn_kv_w, L0_attn_c_w,
                 L0_ffn_router_w, L0_ffn_w1, L0_ffn_w2, L0_ffn_w3):
    bf = ml_dtypes.bfloat16
    x = np.ascontiguousarray(np.asarray(L0_attn_x, dtype=np.float32))
    q_w = np.asarray(L0_attn_q_w, dtype=np.float32)
    kv_w = np.asarray(L0_attn_kv_w, dtype=np.float32)
    c_w = np.asarray(L0_attn_c_w, dtype=np.float32)
    r_w = np.asarray(L0_ffn_router_w, dtype=np.float32)
    w1 = np.ascontiguousarray(np.asarray(L0_ffn_w1, dtype=np.float32))
    w2 = np.ascontiguousarray(np.asarray(L0_ffn_w2, dtype=np.float32))
    w3 = np.ascontiguousarray(np.asarray(L0_ffn_w3, dtype=np.float32))

    perm = np.concatenate([np.arange(0, 128, 2), np.arange(1, 128, 2)])
    q_perm = q_w.reshape(H, 128, D)[:, perm, :].reshape(D, D)
    k_perm = kv_w[0:512].reshape(G4, 128, D)[:, perm, :].reshape(512, D)
    qwT = np.ascontiguousarray(
        q_perm.T.reshape(16, 128, 16, 128).transpose(2, 1, 0, 3))
    kwT = np.ascontiguousarray(
        k_perm.T.reshape(16, 128, 4, 128).transpose(2, 1, 0, 3))
    vwT = np.ascontiguousarray(
        kv_w[512:1024].T.reshape(16, 128, 512).transpose(1, 0, 2))
    cwT = np.ascontiguousarray(
        c_w.T.reshape(16, 128, D).transpose(1, 0, 2))
    rwT = np.ascontiguousarray(
        r_w.T.reshape(16, 128, 8).transpose(1, 0, 2))
    w1 = np.ascontiguousarray(w1.reshape(NE, 16, 128, DFF).transpose(0, 2, 1, 3))
    w2 = np.ascontiguousarray(w2.reshape(NE, 16, 128, DFF).transpose(0, 2, 1, 3))
    w3 = np.ascontiguousarray(w3.reshape(NE, 8, 128, D).transpose(0, 2, 1, 3))

    iden = np.eye(128, dtype=np.float32).astype(bf)
    ones = np.ones((128, 128), dtype=np.float32).astype(bf)
    kk = np.arange(128)[:, None, None]
    jq = np.arange(4)[None, :, None]
    qq = np.arange(512)[None, None, :]
    masks = (qq >= jq * 128 + kk).astype(np.float32).astype(bf)
    i64 = np.arange(64, dtype=np.float64)
    freqs = 1.0 / (10000.0 ** (2.0 * i64 / DK))
    tpos = np.arange(T, dtype=np.float64)
    ang = tpos[None, :] * freqs[:, None]
    ropeC = np.concatenate([np.cos(ang), np.cos(ang)], axis=0).astype(np.float32)
    ropeS = np.concatenate([-np.sin(ang), np.sin(ang)], axis=0).astype(np.float32)

    shared = {"qwT": qwT, "kwT": kwT, "vwT": vwT, "cwT": cwT, "rwT": rwT,
              "w1": w1, "w2": w2, "w3": w3, "iden": iden, "ones": ones,
              "masks": np.ascontiguousarray(masks), "ropeC": ropeC, "ropeS": ropeS}
    in_maps = []
    for b in range(8):
        m = dict(shared)
        m["x"] = np.ascontiguousarray(x[b])
        m["xT"] = np.ascontiguousarray(
            x[b].T.reshape(16, 128, T).transpose(1, 0, 2))
        in_maps.append(m)
    return in_maps


_tap_results = None
_STAGE = 4


def kernel(**inputs):
    global _tap_results
    in_maps = _host_inputs(**inputs)
    nc = build_program(tuple(_DEBUG), _STAGE)
    res = run_bass_kernel_spmd(nc, in_maps, core_ids=list(range(8)))
    if _DEBUG:
        _tap_results = [{k: v for k, v in r.items() if k.startswith("tap_")}
                        for r in res.results]
    return np.stack([np.asarray(res.results[b]["y"], dtype=np.float32)
                     for b in range(8)], axis=0)


# version 06 (final)
# speedup vs baseline: 1.1257x; 1.0153x over previous
import math
import numpy as np
import ml_dtypes
import concourse.bass as bass
import concourse.mybir as mybir
from concourse.bass_utils import run_bass_kernel_spmd

F32 = mybir.dt.float32
BF16 = mybir.dt.bfloat16
U8 = mybir.dt.uint8
AF = mybir.ActivationFunctionType
ALU = mybir.AluOpType
AX = mybir.AxisListType

D = 2048
T = 2048
H = 16
G4 = 4
DK = 128
DFF = 1024
NE = 8
NDT = 16
NTT = 16
EPS = 1e-5
ISQDK = 1.0 / math.sqrt(DK)

_DEBUG = []

OFF_IDEN = 0
OFF_ONES = 256
OFF_MASK = 512
OFF_RWT = 4608
OFF_CGATE = 4864
OFF_STAT = 5120
OFF_EPS = 5376
OFF_DEN = 5440
OFF_DENB = 7488
OFF_RB = 8512
OFF_A = 12672
OFF_B = OFF_A + 65536
OFF_C = OFF_B + 65536
OFF_C2 = OFF_C + 32768
ARENA_BYTES = OFF_C + 65536


class Prog:
    ENGINES = ("tensor", "vector", "scalar", "sync", "gpsimd")
    SEMS = ("T", "V", "S", "G", "Y", "ST")

    def __init__(self):
        self.ops = {e: [] for e in self.ENGINES}
        self.cnt = {s: 0 for s in self.SEMS}

    def add(self, eng, kind, args, waits=None, inc=None, n=1):
        w = {k: v for k, v in (waits or {}).items() if v and v > 0}
        mile = None
        if inc is not None:
            self.cnt[inc] += n
            mile = self.cnt[inc]
        self.ops[eng].append((w, inc, n, kind, args))
        return mile

    def mm(self, out, lhsT, rhs, start, stop, waits=None, inc=None):
        return self.add("tensor", "mm", (out, lhsT, rhs, start, stop), waits, inc)

    def tr(self, out, in_, iden, waits=None, inc=None):
        return self.add("tensor", "tr", (out, in_, iden), waits, inc)

    def tt(self, out, in0, in1, op, waits=None, inc=None):
        return self.add("vector", "tt", (out, in0, in1, op), waits, inc)

    def ts(self, out, in0, s1, s2, op0, op1=None, waits=None, inc=None):
        return self.add("vector", "ts", (out, in0, s1, s2, op0, op1), waits, inc)

    def stt(self, out, in0, sc, in1, op0, op1, waits=None, inc=None):
        return self.add("vector", "stt", (out, in0, sc, in1, op0, op1), waits, inc)

    def red(self, out, in_, op, waits=None, inc=None):
        return self.add("vector", "red", (out, in_, op), waits, inc)

    def vcp(self, out, in_, waits=None, inc=None):
        return self.add("vector", "cp", (out, in_), waits, inc)

    def recip(self, out, in_, waits=None, inc=None):
        return self.add("vector", "recip", (out, in_), waits, inc)

    def vself(self):
        m = self.add("vector", "nopinc", (), None, "V")
        self.add("vector", "wait", (), {"V": m})

    def act(self, out, in_, func, scale=None, bias=None, accum=None, waits=None, inc=None):
        return self.add("scalar", "act", (out, in_, func, scale, bias, accum), waits, inc)

    def scp(self, out, in_, waits=None, inc=None):
        return self.add("scalar", "cp", (out, in_), waits, inc)

    def dma(self, q, out, in_, waits=None, inc=None):
        return self.add(q, "dma", (out, in_), waits, inc, n=16)

    def mset(self, ap, val, waits=None, inc=None):
        return self.add("vector", "mset", (ap, val), waits, inc)

    def fin(self, eng, waits):
        self.add(eng, "wait", (), waits)

    def replay(self, eng, name, sems):
        for (w, inc, n, kind, a) in self.ops[name]:
            for s, v in w.items():
                eng.wait_ge(sems[s], v)
            r = None
            if kind == "mm":
                r = eng.matmul(a[0], a[1], a[2], start=a[3], stop=a[4])
            elif kind == "tr":
                r = eng.transpose(a[0], a[1], a[2])
            elif kind == "tt":
                r = eng.tensor_tensor(a[0], a[1], a[2], op=a[3])
            elif kind == "ts":
                r = eng.tensor_scalar(a[0], a[1], a[2], a[3], op0=a[4],
                                      op1=(a[5] if a[5] is not None else ALU.bypass))
            elif kind == "stt":
                r = eng.scalar_tensor_tensor(a[0], a[1], a[2], a[3], op0=a[4], op1=a[5])
            elif kind == "red":
                r = eng.tensor_reduce(a[0], a[1], axis=AX.X, op=a[2])
            elif kind == "cp":
                r = eng.tensor_copy(a[0], a[1]) if name == "vector" else eng.copy(a[0], a[1])
            elif kind == "recip":
                r = eng.reciprocal(a[0], a[1])
            elif kind == "act":
                kwargs = {}
                if a[3] is not None:
                    kwargs["scale"] = a[3]
                if a[4] is not None:
                    kwargs["bias"] = a[4]
                if a[5] is not None:
                    kwargs["accum_out"] = a[5]
                r = eng.activation(a[0], a[1], a[2], **kwargs)
            elif kind == "dma":
                r = eng.dma_start(out=a[0], in_=a[1])
            elif kind == "mset":
                r = eng.memset(a[0], a[1])
            elif kind == "nopinc":
                r = None
            elif kind == "wait":
                continue
            if inc is not None:
                if r is not None:
                    r.then_inc(sems[inc], n)
                else:
                    eng.sem_inc(sems[inc], n)


def _r(ap, pattern, **kw):
    return ap.rearrange(pattern, **kw)


def build_program(debug_taps=(), stage=4):
    nc = bass.Bass("TRN2", target_bir_lowering=False)

    x_in = nc.dram_tensor("x", [T, D], F32, kind="ExternalInput").ap()
    xT_in = nc.dram_tensor("xT", [128, NDT, T], F32, kind="ExternalInput").ap()
    qwT_in = nc.dram_tensor("qwT", [16, 128, NDT, 128], F32, kind="ExternalInput").ap()
    kwT_in = nc.dram_tensor("kwT", [4, 128, NDT, 128], F32, kind="ExternalInput").ap()
    vwT_in = nc.dram_tensor("vwT", [128, NDT, 512], F32, kind="ExternalInput").ap()
    cwT_in = nc.dram_tensor("cwT", [128, NDT, D], F32, kind="ExternalInput").ap()
    rwT_in = nc.dram_tensor("rwT", [128, NDT, 8], F32, kind="ExternalInput").ap()
    w1_in = nc.dram_tensor("w1", [NE, 128, NDT, DFF], F32, kind="ExternalInput").ap()
    w2_in = nc.dram_tensor("w2", [NE, 128, NDT, DFF], F32, kind="ExternalInput").ap()
    w3_in = nc.dram_tensor("w3", [NE, 128, 8, D], F32, kind="ExternalInput").ap()
    iden_in = nc.dram_tensor("iden", [128, 128], BF16, kind="ExternalInput").ap()
    ones_in = nc.dram_tensor("ones", [128, 128], BF16, kind="ExternalInput").ap()
    mask_in = nc.dram_tensor("masks", [128, 4, 512], BF16, kind="ExternalInput").ap()
    ropeC_in = nc.dram_tensor("ropeC", [128, T], F32, kind="ExternalInput").ap()
    ropeS_in = nc.dram_tensor("ropeS", [128, T], F32, kind="ExternalInput").ap()
    y_out = nc.dram_tensor("y", [T, D], F32, kind="ExternalOutput").ap()
    x1f_d = nc.dram_tensor("x1f", [NTT, 128, D], F32, kind="Internal").ap()
    taps = {}
    for t_ in debug_taps:
        shp = {"qt": [128, H * T], "kt": [128, G4 * T], "v": [128, NTT * 512],
               "ot": [128, H * T], "x1": [T, D], "expt": [128, 16 * 512],
               "den": [128, 512], "rbs": [128, 512], "cg": [128, 64],
               "hbuf": [128, 8 * 1024]}[t_]
        dt_ = F32 if t_ in ("x1", "den", "rbs", "cg") else BF16
        taps[t_] = nc.dram_tensor("tap_" + t_, shp, dt_, kind="ExternalOutput").ap()

    from contextlib import ExitStack
    with ExitStack() as ctx:
        en = ctx.enter_context
        arena = en(nc.sbuf_tensor("arena", [128, ARENA_BYTES], U8))
        af = arena.bitcast(F32)
        ab = arena.bitcast(BF16)
        pbig = en(nc.psum_tensor("pbig", [128, 4096], F32))
        pbig_bf = pbig.bitcast(BF16)

        def fsl(off, n, p=slice(None)):
            return af[p, off // 4: off // 4 + n]

        def bsl(off, n, p=slice(None)):
            return ab[p, off // 2: off // 2 + n]

        iden = bsl(OFF_IDEN, 128)
        onesb = bsl(OFF_ONES, 128)
        scrS = fsl(5392, 2, p=slice(0, 1))
        scrV = fsl(5408, 2, p=slice(0, 1))
        masks = bsl(OFF_MASK, 4 * 512)
        rwT = bsl(OFF_RWT, 16 * 8)
        rwT3 = _r(rwT, "p (dt e) -> p dt e", e=8)
        cgate = fsl(OFF_CGATE, 64)
        stat = fsl(OFF_STAT, 64)
        epsv = fsl(OFF_EPS, 1)
        sden = fsl(OFF_DEN, 512, p=slice(0, 1))
        sdenb = bsl(OFF_DENB, 512, p=slice(0, 1))
        rb = [fsl(OFF_RB, 512), fsl(OFF_RB + 2048, 512)]
        A_bf = bsl(OFF_A, 32768)
        A_xT = _r(A_bf, "p (dt t) -> p dt t", t=T)
        A_cwT = _r(A_bf, "p (dt o) -> p dt o", o=D)
        A_x1T = _r(bsl(OFF_A, 16 * 1024), "p (dt t) -> p dt t", t=1024)
        A_silu = bsl(OFF_A + 32768, 8 * 1024)
        A_h = bsl(OFF_A + 49152, 8 * 1024)
        B_bf = bsl(OFF_B, 32768)
        wbuf = [_r(bsl(OFF_B, 16 * 1024), "p (dt f) -> p dt f", f=1024),
                _r(bsl(OFF_B + 32768, 16 * 1024), "p (dt f) -> p dt f", f=1024)]
        KT = bsl(OFF_C, 4 * T)
        Vt = bsl(OFF_C + 16384, 16 * 512)
        ropeC = fsl(OFF_C2, T)
        ropeS = fsl(OFF_C2 + 8192, T)
        tmp1 = fsl(OFF_C2 + 16384, 1024)
        tmp2 = fsl(OFF_C2 + 20480, 1024)
        wstream = [bsl(OFF_C2 + 24576, 2048), bsl(OFF_C2 + 28672, 2048)]
        vwTb = _r(bsl(OFF_C2 + 16384, 16 * 512), "p (dt o) -> p dt o", o=512)
        if globals().get("_DBG_RING", False):
            expT = bsl(OFF_A + 32768, 16 * 512)
        else:
            expT = bsl(OFF_C2, 16 * 512)
        xres = [fsl(OFF_C2 + 16384, 2048), fsl(OFF_C2 + 24576, 2048)]
        sscr2 = [fsl(OFF_C, 2048), fsl(OFF_C + 8192, 2048)]
        lndump = fsl(OFF_C + 16384, 2048)
        x1b = bsl(OFF_C2 + 16384, 2048)
        x1stage = [bsl(OFF_C2 + 20480, 2048), bsl(OFF_C2 + 24576, 2048)]
        acc = fsl(OFF_C, 8 * 2048)
        ln2dump = fsl(OFF_MASK, 1024)

        P = Prog()

        def vflush(src):
            return P.vcp(scrV[:, 0:1], src[0:1, 0:1], inc="V")

        def sflush(src):
            return P.scp(scrS[:, 0:1], src[0:1, 0:1], inc="S")

        def _finish():
            P.fin("sync", {"ST": P.cnt["ST"]})
            sems = {}
            for s in Prog.SEMS:
                sems[s] = en(nc.semaphore("sem" + s))
            block = en(nc.Block())

            @block.tensor
            def _(eng):
                P.replay(eng, "tensor", sems)

            @block.vector
            def _(eng):
                P.replay(eng, "vector", sems)

            @block.scalar
            def _(eng):
                P.replay(eng, "scalar", sems)

            @block.sync
            def _(eng):
                P.replay(eng, "sync", sems)

            @block.gpsimd
            def _(eng):
                P.replay(eng, "gpsimd", sems)

        P.dma("sync", iden, iden_in, inc="Y")
        P.dma("sync", onesb, ones_in, inc="Y")
        P.dma("sync", _r(masks, "p (a b) -> p a b", a=4), mask_in, inc="Y")
        P.dma("sync", ropeC, ropeC_in, inc="Y")
        yconst = P.dma("sync", ropeS, ropeS_in, inc="Y")
        P.mset(epsv, EPS, inc="V")

        g_rwT = P.dma("gpsimd", rwT3, rwT_in, inc="G")
        g_w0 = P.dma("gpsimd",
                     _r(wstream[0], "p (dt o) -> p dt o", o=128),
                     _r(qwT_in[0:1], "a p dt o -> p (a dt) o"), inc="G")
        g_xT = P.dma("gpsimd", A_xT, xT_in, inc="G")
        P.dma("gpsimd", bsl(OFF_B, 2048), xT_in[:, 0, 0:2048], inc="G")
        g_canary = P.dma("gpsimd", fsl(OFF_DEN, 16), xT_in[:, 0, 0:16], inc="G")

        NQO = 20
        g_wtile = {0: g_w0}
        proj_done = {}
        rope_evac = {}
        for ot in range(NQO):
            if ot > 0:
                src = qwT_in[ot:ot + 1] if ot < 16 else kwT_in[ot - 16:ot - 15]
                w = {"T": proj_done.get(ot - 2, 0)} if ot >= 2 else None
                g_wtile[ot] = P.dma("gpsimd",
                                    _r(wstream[ot % 2], "p (dt o) -> p dt o", o=128),
                                    _r(src, "a p dt o -> p (a dt) o"), waits=w, inc="G")
            par = ot % 2
            pq = pbig[:, par * 2048:(par + 1) * 2048]
            buf3 = _r(wstream[ot % 2], "p (dt o) -> p dt o", o=128)
            gw = max(g_xT, g_wtile[ot])
            if ot == 0 and g_canary:
                gw = max(gw, g_canary)
            w0 = {"G": gw, "V": rope_evac.get(ot - 2, 0)}
            for dt in range(NDT):
                for nch in range(4):
                    last = (dt == NDT - 1 and nch == 3)
                    P.mm(pq[:, nch * 512:(nch + 1) * 512], buf3[:, dt, :],
                         A_xT[:, dt, nch * 512:(nch + 1) * 512],
                         start=(dt == 0), stop=(dt == NDT - 1),
                         waits=(w0 if (dt == 0 and nch == 0) else None),
                         inc=("T" if last else None))
            proj_done[ot] = P.cnt["T"]
            dst = bsl(OFF_B + ot * 4096, T) if ot < 16 else bsl(OFF_C + (ot - 16) * 4096, T)
            wv = {"T": proj_done[ot], "Y": yconst}
            for ch in range(2):
                cs = slice(ch * 1024, (ch + 1) * 1024)
                P.tt(tmp1, pq[:, cs], ropeC[:, cs], ALU.mult, waits=(wv if ch == 0 else None))
                P.tt(tmp2[0:64, :], pq[64:128, cs], ropeS[0:64, cs], ALU.mult)
                P.tt(tmp2[64:128, :], pq[0:64, cs], ropeS[64:128, cs], ALU.mult)
                P.vself()
                P.tt(dst[:, cs], tmp1, tmp2, ALU.add, inc=("V" if ch == 1 else None))
            rope_evac[ot] = P.cnt["V"]
        rope_all = P.cnt["V"]

        P.dma("gpsimd", vwTb, vwT_in, waits={"V": rope_evac[19]}, inc="G")
        P.dma("gpsimd", bsl(OFF_DENB, 512), vwT_in[:, 0, 0:512], inc="G")
        g_vwT = P.dma("gpsimd", fsl(OFF_DEN, 16), vwT_in[:, 0, 0:16], inc="G")
        vevac = {}
        for tt in range(NTT):
            pv = pbig[:, (tt % 2) * 2048:(tt % 2) * 2048 + 512]
            w0 = {"G": g_vwT, "V": rope_all, "S": vevac.get(tt - 2, 0)}
            for dt in range(NDT):
                P.mm(pv, A_xT[:, dt, tt * 128:(tt + 1) * 128], vwTb[:, dt, :],
                     start=(dt == 0), stop=(dt == NDT - 1),
                     waits=(w0 if dt == 0 else None),
                     inc=("T" if dt == NDT - 1 else None))
            m = P.cnt["T"]
            vevac[tt] = P.scp(Vt[:, tt * 512:(tt + 1) * 512], pv, waits={"T": m}, inc="S")
        p1_T = P.cnt["T"]
        p1_S = P.cnt["S"]

        if "qt" in taps:
            P.dma("sync", taps["qt"], B_bf, waits={"V": rope_all}, inc="ST")
        if "kt" in taps:
            P.dma("sync", taps["kt"], KT, waits={"V": rope_all}, inc="ST")
        if "v" in taps:
            P.dma("sync", taps["v"], Vt, waits={"S": p1_S}, inc="ST")

        if stage < 2:
            _finish()
            return nc

        g_cwT = P.dma("gpsimd", A_cwT, cwT_in, waits={"T": p1_T}, inc="G")

        prev_evac = [0, 0]
        prev_rbm = [0, 0]
        lastexp = [0, 0, 0, 0]
        exp_mile = {}
        pend = {}

        def flush_tail():
            if not pend:
                return
            par, h_, qs_, pv_stop, exp_last, recip_m = (
                pend["par"], pend["h"], pend["qs"],
                pend["pv_stop"], pend["exp_last"], pend["recip_m"])
            dbank = pbig[:, (6 + par) * 512: (7 + par) * 512]
            obank = pbig[:, (4 + par) * 512: (5 + par) * 512]
            bc_m = P.mm(dbank, onesb[0:1, :], sdenb, True, True,
                        waits={"V": recip_m, "S": exp_last}, inc="T")
            P.scp(rb[par], dbank, waits={"T": bc_m, "V": prev_evac[par]}, inc="S")
            prev_rbm[par] = sflush(rb[par])
            prev_evac[par] = P.tt(bsl(OFF_B + h_ * 4096, T)[:, qs_], obank, rb[par],
                                  ALU.mult, waits={"S": prev_rbm[par], "T": pv_stop},
                                  inc="V")
            pend.clear()

        _H2 = globals().get("_P2H", H)
        _C2n = globals().get("_P2C", 4)
        _horder = globals().get("_HORDER", None) or list(range(_H2))
        first_q = True
        for h in _horder:
            g = h // 4
            QTh = bsl(OFF_B + h * 4096, T)
            for c in range(_C2n):
                par = (h * 4 + c) % 2
                jmax = 4 * c + 3
                qs = slice(c * 512, (c + 1) * 512)
                lt_mile = {}
                mask_last = 0
                lt_bank = lambda j: pbig[:, (j % 4) * 512: (j % 4) * 512 + 512]
                esl = lambda j: expT[:, j * 512:(j + 1) * 512]
                obank = pbig[:, (4 + par) * 512: (5 + par) * 512]
                dbank = pbig[0:1, (6 + par) * 512: (7 + par) * 512]

                def lt(j):
                    w = {"S": lastexp[j % 4]}
                    if first_q and j == 0:
                        w["V"] = rope_all
                        w["S"] = max(w["S"], p1_S)
                    lt_mile[j] = P.mm(lt_bank(j),
                                      KT[:, g * T + j * 128: g * T + (j + 1) * 128],
                                      QTh[:, qs], True, True, waits=w, inc="T")

                def den_pv(j, dwait):
                    if j == 0:
                        dwait = dict(dwait)
                        dwait["S"] = max(dwait.get("S", 0), prev_rbm[par])
                        dwait["V"] = max(dwait.get("V", 0), prev_evac[par])
                    P.mm(dbank, onesb[:, 0:1], esl(j), j == 0, j == jmax,
                         waits=dwait, inc=("T" if j == jmax else None))
                    P.mm(obank, Vt[:, j * 512 + g * 128: j * 512 + g * 128 + 128],
                         esl(j), j == 0, j == jmax,
                         inc=("T" if j == jmax else None))

                lt(0)
                first_q = False
                mask_m = {}
                for j in range(jmax + 1):
                    if j + 1 <= jmax:
                        lt(j + 1)
                    exp_mile[(h, c, j)] = P.act(esl(j), lt_bank(j),
                                                AF.Exp, scale=ISQDK,
                                                waits={"T": lt_mile[j]}, inc="S")
                    lastexp[j % 4] = exp_mile[(h, c, j)]
                    if j >= 4 * c:
                        mask_last = P.tt(esl(j), esl(j),
                                         masks[:, (j - 4 * c) * 512:(j - 4 * c + 1) * 512],
                                         ALU.mult, waits={"S": exp_mile[(h, c, j)]}, inc="V")
                    mask_m[j] = mask_last
                    if j >= 2:
                        den_pv(j - 2, {"S": exp_mile[(h, c, j - 1)], "V": mask_m[j - 1]})
                    if j == 3:
                        flush_tail()
                den_pv(jmax - 1, {"S": exp_mile[(h, c, jmax)], "V": mask_m[jmax]})
                sf = sflush(esl(jmax))
                vf = vflush(esl(jmax))
                den_pv(jmax, {"S": sf, "V": vf})
                pv_stop = P.cnt["T"]
                den_stop = pv_stop - 1
                P.vcp(sden, dbank, waits={"T": den_stop}, inc="V")
                P.add("vector", "wait", (), {"V": P.cnt["V"]})
                P.recip(sden, sden, inc="V")
                P.add("vector", "wait", (), {"V": P.cnt["V"]})
                P.vcp(sdenb, sden, inc="V")
                P.add("vector", "wait", (), {"V": P.cnt["V"]})
                pend.update(par=par, h=h, qs=qs, pv_stop=pv_stop,
                            exp_last=exp_mile[(h, c, jmax)], recip_m=vflush(sdenb))
        flush_tail()
        p2_T = P.cnt["T"]
        vflush(bsl(OFF_B + (H - 1) * 4096, T))
        p2_V = P.cnt["V"]


        if "ot" in taps:
            P.dma("sync", taps["ot"], B_bf, waits={"V": p2_V}, inc="ST")
        if "expt" in taps:
            P.dma("sync", taps["expt"], expT, waits={"V": p2_V, "S": P.cnt["S"]}, inc="ST")
        if "den" in taps:
            P.dma("sync", taps["den"], fsl(OFF_DEN, 512), waits={"V": p2_V}, inc="ST")
        if "rbs" in taps:
            P.dma("sync", taps["rbs"], rb[0], waits={"V": p2_V, "S": P.cnt["S"]}, inc="ST")
        if stage < 3:
            _finish()
            return nc

        xres_mile = {}
        resid_add = {}
        norm_mile = {}
        x1b_mile = {}
        ttev = {}
        tr_done = {}
        x1f_store = {}
        stage_store = {}
        for tt in range(NTT):
            s3 = tt % 2
            base = s3 * 2048
            w = {"T": p1_T} if tt < 2 else {"V": resid_add[tt - 2]}
            xres_mile[tt] = P.dma("sync", xres[tt % 2], x_in[tt * 128:(tt + 1) * 128, :],
                                  waits=w, inc="Y")
            w0 = {"G": g_cwT, "V": p2_V}
            if tt >= 2:
                w0["V"] = max(w0["V"], resid_add[tt - 2])
            for dt in range(NDT):
                for nch in range(4):
                    last = (dt == NDT - 1 and nch == 3)
                    P.mm(pbig[:, base + nch * 512: base + (nch + 1) * 512],
                         bsl(OFF_B + dt * 4096, T)[:, tt * 128:(tt + 1) * 128],
                         A_cwT[:, dt, nch * 512:(nch + 1) * 512],
                         start=(dt == 0), stop=(dt == NDT - 1),
                         waits=(w0 if (dt == 0 and nch == 0) else None),
                         inc=("T" if last else None))
            cp_m = P.cnt["T"]
            sscr = sscr2[tt % 2]
            P.tt(sscr, pbig[:, base: base + 2048], xres[tt % 2], ALU.add,
                 waits={"T": cp_m, "Y": max(xres_mile[tt],
                                            x1f_store.get(tt - 2, 0))},
                 inc="V")
            resid_add[tt] = P.cnt["V"]
            ra_f = vflush(sscr)
            sc = (tt % 4) * 16
            P.act(lndump, sscr, AF.Identity, accum=stat[:, sc:sc + 1],
                  waits={"V": ra_f})
            st_m = P.act(lndump, sscr, AF.Square, accum=stat[:, sc + 1:sc + 2], inc="S")
            st_m = sflush(stat[:, sc:sc + 2])
            P.ts(stat[:, sc + 2:sc + 3], stat[:, sc:sc + 1], 1.0 / D, None, ALU.mult,
                 waits={"S": st_m})
            P.ts(stat[:, sc + 3:sc + 4], stat[:, sc + 1:sc + 2], 1.0 / D, None, ALU.mult)
            P.vself()
            P.tt(stat[:, sc + 4:sc + 5], stat[:, sc + 2:sc + 3], stat[:, sc + 2:sc + 3], ALU.mult)
            P.vself()
            P.tt(stat[:, sc + 5:sc + 6], stat[:, sc + 3:sc + 4], stat[:, sc + 4:sc + 5],
                 ALU.subtract)
            var_m = vflush(stat[:, sc + 5:sc + 6])
            P.act(stat[:, sc + 6:sc + 7], stat[:, sc + 5:sc + 6], AF.Sqrt, bias=epsv,
                  waits={"V": var_m}, inc="S")
            sq_m = sflush(stat[:, sc + 6:sc + 7])
            P.recip(stat[:, sc + 7:sc + 8], stat[:, sc + 6:sc + 7], waits={"S": sq_m})
            P.vself()
            norm_mile[tt] = P.ts(sscr, sscr, stat[:, sc + 2:sc + 3], stat[:, sc + 7:sc + 8],
                                 ALU.subtract, ALU.mult, inc="V")
            nf = vflush(sscr)
            x1f_store[tt] = P.dma("sync", x1f_d[tt], sscr, waits={"V": nf}, inc="Y")
            if "x1" in taps:
                P.dma("sync", taps["x1"][tt * 128:(tt + 1) * 128, :], sscr, inc="ST")
        p3_T = P.cnt["T"]
        p3_V = P.cnt["V"]
        p3_S = P.cnt["S"]
        if stage < 4:
            _finish()
            return nc

        wload = {}
        sweep_done = {}
        h_mile_prev = {}
        prev_silu7 = 0
        p4v_prev = 0
        y_st_total = 0
        for hf in range(2):
            wacc = {"V": p3_V, "S": p3_S, "T": p3_T} if hf == 0 else \
                {"V": p4v_prev, "ST": y_st_total, "T": sweep_done.get(hf * 24 - 1, 0)}
            P.dma("sync", _r(acc, "p (tt d) -> p tt d", d=D),
                  _r(x1f_d[hf * 8:(hf + 1) * 8], "tt p d -> p tt d"),
                  waits=wacc, inc="Y")
            P.dma("sync", fsl(OFF_DEN, 512), x1f_d[0][0:128, 0:512], inc="Y")
            acc_ld = P.dma("sync", fsl(OFF_DEN, 16), x1f_d[0][0:128, 0:16], inc="Y")
            x1b4 = bsl(OFF_A + 32768, 2048)
            trev = {}
            tr_t = {}
            for tt in range(8):
                s4 = tt % 2
                wv = {"Y": acc_ld}
                if tt >= 1:
                    wv["T"] = tr_t[tt - 1]
                P.vcp(x1b4, acc[:, tt * 2048:(tt + 1) * 2048], waits=wv, inc="V")
                x1b4_m = vflush(x1b4)
                for dt in range(NDT):
                    bko = (s4 * 2048 + (dt // 4) * 512) * 2 + (dt % 4) * 128
                    w = None
                    if dt == 0:
                        w = {"V": x1b4_m}
                        if tt >= 2:
                            w["V"] = max(w["V"], trev[(tt - 2, 3)])
                    P.tr(pbig_bf[:, bko: bko + 128], x1b4[:, dt * 128:(dt + 1) * 128],
                         iden, waits=w, inc=("T" if dt % 4 == 3 else None))
                    if dt % 4 == 3:
                        grp = dt // 4
                        trev[(tt, grp)] = P.vcp(
                            A_x1T[:, grp * 4:(grp + 1) * 4, tt * 128:(tt + 1) * 128],
                            _r(pbig_bf[:, (s4 * 2048 + grp * 512) * 2:
                                       (s4 * 2048 + grp * 512) * 2 + 512],
                               "p (a b) -> p a b", b=128),
                            waits={"T": P.cnt["T"]}, inc="V")
                tr_t[tt] = P.cnt["T"]
            x1t_gate = vflush(bsl(OFF_A, 4))
            x1t_ld = acc_ld
            def emit_wload(k):
                e_, m_ = (k % 24) // 3, (k % 24) % 3
                src = (w1_in, w2_in, w3_in)[m_]
                pat = "a p dt f -> p (a dt) f" if m_ < 2 else "a p ft d -> p (a ft) d"
                s_ap = _r(src[e_:e_ + 1], pat)
                w = {"T": sweep_done.get(k - 2, p3_T if k < 2 else 0)}
                dst = wbuf[k % 2] if m_ < 2 else \
                    _r(bsl(OFF_B + (k % 2) * 32768, 16 * 1024), "p (ft d) -> p ft d", d=D)
                wload[k] = P.dma("gpsimd", dst, s_ap, waits=w, inc="G")
                if k % 24 < 3:
                    P.dma("gpsimd", bsl(OFF_DENB, 512), xT_in[:, 0, 0:512], inc="G")
                    wload[k] = P.dma("gpsimd", fsl(OFF_DEN, 16),
                                     xT_in[:, 0, 0:16], inc="G")
                return wload[k]

            emit_wload(hf * 24 + 0)
            emit_wload(hf * 24 + 1)
            lgcp = {}
            cg_mile = {}
            lg = stat[:, 16:24]
            prt = pbig[:, 2048:2056]
            for ht in range(8):
                w0 = {"Y": x1t_ld, "G": g_rwT,
                      "V": max(lgcp.get(ht - 1, p4v_prev), x1t_gate)}
                for dt in range(NDT):
                    P.mm(prt, A_x1T[:, dt, ht * 128:(ht + 1) * 128], rwT3[:, dt, :],
                         start=(dt == 0), stop=(dt == NDT - 1),
                         waits=(w0 if dt == 0 else None),
                         inc=("T" if dt == NDT - 1 else None))
                rt_m = P.cnt["T"]
                lgcp[ht] = P.vcp(lg, prt, waits={"T": rt_m}, inc="V")
                P.vself()
                P.red(stat[:, 24:25], lg, ALU.max)
                P.vself()
                P.ts(stat[:, 32:40], lg, stat[:, 24:25], None, ALU.is_ge)
                P.vself()
                P.stt(stat[:, 40:48], stat[:, 32:40], -1e30, lg, ALU.mult, ALU.add)
                P.vself()
                P.red(stat[:, 25:26], stat[:, 40:48], ALU.max)
                P.vself()
                P.ts(stat[:, 48:56], stat[:, 40:48], stat[:, 25:26], None, ALU.is_ge)
                P.tt(stat[:, 26:27], stat[:, 24:25], stat[:, 25:26], ALU.subtract)
                d12_m = vflush(stat[:, 26:27])
                P.act(stat[:, 27:28], stat[:, 26:27], AF.Exp, scale=-1.0,
                      waits={"V": d12_m}, inc="S")
                e12_m = sflush(stat[:, 27:28])
                P.ts(stat[:, 28:29], stat[:, 27:28], 1.0, None, ALU.add, waits={"S": e12_m})
                P.vself()
                P.recip(stat[:, 29:30], stat[:, 28:29])
                P.vself()
                P.ts(stat[:, 30:31], stat[:, 29:30], -1.0, 1.0, ALU.mult, ALU.add)
                P.vself()
                P.ts(stat[:, 56:64], stat[:, 48:56], stat[:, 30:31], None, ALU.mult)
                cg_mile[ht] = P.stt(cgate[:, ht * 8:(ht + 1) * 8], stat[:, 32:40],
                                    stat[:, 29:30], stat[:, 56:64], ALU.mult, ALU.add,
                                    inc="V")
            accev = {}
            _NE = globals().get("_P4E", NE)
            for e in range(_NE):
                k1, k2, k3 = hf * 24 + 3 * e, hf * 24 + 3 * e + 1, hf * 24 + 3 * e + 2
                silu_mile = {}
                v_mile = {}
                h_mile = {}
                g_mile = {}
                for ft in range(8):
                    par = ft % 2
                    pg = pbig[:, par * 1024:(par + 1) * 1024]
                    sgate = silu_mile.get(ft - 2, 0)
                    if ft < 2:
                        sgate = max(sgate, prev_silu7)
                    w0 = {"G": wload[k1], "S": sgate,
                          "V": max(cg_mile[7], x1t_gate)}
                    if e > 0:
                        w0["V"] = max(w0["V"], accev.get((e - 1, 7), 0))
                    for dt in range(NDT):
                        for ch in range(2):
                            last = (dt == NDT - 1 and ch == 1)
                            P.mm(pg[:, ch * 512:(ch + 1) * 512],
                                 wbuf[k1 % 2][:, dt, ft * 128:(ft + 1) * 128],
                                 A_x1T[:, dt, ch * 512:(ch + 1) * 512],
                                 start=(dt == 0), stop=(dt == NDT - 1),
                                 waits=(w0 if (dt == 0 and ch == 0) else None),
                                 inc=("T" if last else None))
                    g_mile[ft] = P.cnt["T"]
                    silu_mile[ft] = P.act(A_silu[:, ft * 1024:(ft + 1) * 1024],
                                          pg, AF.Silu,
                                          waits={"T": g_mile[ft],
                                                 "V": h_mile_prev.get(ft, 0)}, inc="S")
                sweep_done[k1] = P.cnt["T"]
                emit_wload(k1 + 2)
                for ft in range(8):
                    par = ft % 2
                    pv_ = pbig[:, par * 1024:(par + 1) * 1024]
                    w0 = {"G": wload[k2], "S": silu_mile[7],
                          "V": h_mile.get(ft - 2, 0)}
                    for dt in range(NDT):
                        for ch in range(2):
                            last = (dt == NDT - 1 and ch == 1)
                            P.mm(pv_[:, ch * 512:(ch + 1) * 512],
                                 wbuf[k2 % 2][:, dt, ft * 128:(ft + 1) * 128],
                                 A_x1T[:, dt, ch * 512:(ch + 1) * 512],
                                 start=(dt == 0), stop=(dt == NDT - 1),
                                 waits=(w0 if (dt == 0 and ch == 0) else None),
                                 inc=("T" if last else None))
                    vm = P.cnt["T"]
                    v_mile[ft] = vm
                    h_mile[ft] = P.tt(A_h[:, ft * 1024:(ft + 1) * 1024], pv_,
                                      A_silu[:, ft * 1024:(ft + 1) * 1024], ALU.mult,
                                      waits={"T": vm}, inc="V")
                    if ft == 7:
                        h_mile[7] = vflush(A_h)
                sweep_done[k2] = P.cnt["T"]
                if k2 + 2 <= 47:
                    emit_wload(k2 + 2)
                h_mile_prev = dict(h_mile)
                prev_silu7 = silu_mile[7]
                w3b = _r(bsl(OFF_B + (k3 % 2) * 32768, 16 * 1024), "p (ft d) -> p ft d", d=D)
                for tt in range(8):
                    po = pbig[:, 2048:4096] if tt % 2 == 0 else pbig[:, 0:2048]
                    pob = 2048 if tt % 2 == 0 else 0
                    w0 = {"G": wload[k3], "V": max(h_mile[7], accev.get((e, tt - 2), 0))}
                    for ft in range(8):
                        for ch in range(4):
                            last = (ft == 7 and ch == 3)
                            P.mm(pbig[:, pob + ch * 512: pob + (ch + 1) * 512],
                                 _r(A_h, "p (ft t) -> p ft t", t=1024)[:, ft, tt * 128:(tt + 1) * 128],
                                 w3b[:, ft, ch * 512:(ch + 1) * 512],
                                 start=(ft == 0), stop=(ft == 7),
                                 waits=(w0 if (ft == 0 and ch == 0) else None),
                                 inc=("T" if last else None))
                    w3_m = P.cnt["T"]
                    P.stt(acc[:, tt * 2048:(tt + 1) * 2048],
                          pbig[:, pob: pob + 2048],
                          cgate[:, tt * 8 + e: tt * 8 + e + 1],
                          acc[:, tt * 2048:(tt + 1) * 2048],
                          ALU.mult, ALU.add,
                          waits={"T": w3_m, "Y": acc_ld}, inc="V")
                    accev[(e, tt)] = (vflush(acc[:, tt * 2048:(tt + 1) * 2048])
                                      if e == NE - 1 else P.cnt["V"])
                sweep_done[k3] = P.cnt["T"]
                if k3 + 2 <= 47:
                    emit_wload(k3 + 2)
            for tt in range(8):
                a_t = acc[:, tt * 2048:(tt + 1) * 2048]
                sc = (tt % 2) * 16
                P.act(ln2dump, a_t[:, 0:1024], AF.Identity, accum=stat[:, sc:sc + 1],
                      waits={"V": accev.get((_NE - 1, tt), 0),
                             "Y": acc_ld, "S": prev_silu7})
                P.act(ln2dump, acc[:, tt * 2048 + 1024:(tt + 1) * 2048], AF.Identity,
                      accum=stat[:, sc + 1:sc + 2])
                P.act(ln2dump, a_t[:, 0:1024], AF.Square, accum=stat[:, sc + 2:sc + 3])
                st_m = P.act(ln2dump, acc[:, tt * 2048 + 1024:(tt + 1) * 2048], AF.Square,
                             accum=stat[:, sc + 3:sc + 4], inc="S")
                P.tt(stat[:, sc + 4:sc + 5], stat[:, sc:sc + 1], stat[:, sc + 1:sc + 2],
                     ALU.add, waits={"S": st_m})
                P.tt(stat[:, sc + 5:sc + 6], stat[:, sc + 2:sc + 3], stat[:, sc + 3:sc + 4],
                     ALU.add)
                P.vself()
                P.ts(stat[:, sc + 6:sc + 7], stat[:, sc + 4:sc + 5], 1.0 / D, None, ALU.mult)
                P.ts(stat[:, sc + 7:sc + 8], stat[:, sc + 5:sc + 6], 1.0 / D, None, ALU.mult)
                P.vself()
                P.tt(stat[:, sc + 8:sc + 9], stat[:, sc + 6:sc + 7], stat[:, sc + 6:sc + 7],
                     ALU.mult)
                P.vself()
                P.tt(stat[:, sc + 9:sc + 10], stat[:, sc + 7:sc + 8],
                     stat[:, sc + 8:sc + 9], ALU.subtract)
                var_m = vflush(stat[:, sc + 9:sc + 10])
                P.act(stat[:, sc + 10:sc + 11], stat[:, sc + 9:sc + 10], AF.Sqrt,
                      bias=epsv, waits={"V": var_m}, inc="S")
                sq_m = sflush(stat[:, sc + 10:sc + 11])
                P.recip(stat[:, sc + 11:sc + 12], stat[:, sc + 10:sc + 11], waits={"S": sq_m})
                P.vself()
                P.ts(a_t, a_t, stat[:, sc + 6:sc + 7], stat[:, sc + 11:sc + 12],
                     ALU.subtract, ALU.mult, inc="V")
                n_m = vflush(a_t)
                P.dma("sync", y_out[(hf * 8 + tt) * 128:(hf * 8 + tt + 1) * 128, :], a_t,
                      waits={"V": n_m}, inc="ST")
                y_st_total = P.cnt["ST"]
            p4v_prev = P.cnt["V"]

        if "cg" in taps:
            P.dma("sync", taps["cg"], cgate, waits={"V": P.cnt["V"]}, inc="ST")
        if "hbuf" in taps:
            P.dma("sync", taps["hbuf"], A_h, waits={"V": P.cnt["V"]}, inc="ST")
        _finish()
    return nc


def _host_inputs(L0_attn_x, L0_attn_q_w, L0_attn_kv_w, L0_attn_c_w,
                 L0_ffn_router_w, L0_ffn_w1, L0_ffn_w2, L0_ffn_w3):
    bf = ml_dtypes.bfloat16
    x = np.ascontiguousarray(np.asarray(L0_attn_x, dtype=np.float32))
    q_w = np.asarray(L0_attn_q_w, dtype=np.float32)
    kv_w = np.asarray(L0_attn_kv_w, dtype=np.float32)
    c_w = np.asarray(L0_attn_c_w, dtype=np.float32)
    r_w = np.asarray(L0_ffn_router_w, dtype=np.float32)
    w1 = np.ascontiguousarray(np.asarray(L0_ffn_w1, dtype=np.float32))
    w2 = np.ascontiguousarray(np.asarray(L0_ffn_w2, dtype=np.float32))
    w3 = np.ascontiguousarray(np.asarray(L0_ffn_w3, dtype=np.float32))

    perm = np.concatenate([np.arange(0, 128, 2), np.arange(1, 128, 2)])
    q_perm = q_w.reshape(H, 128, D)[:, perm, :].reshape(D, D)
    k_perm = kv_w[0:512].reshape(G4, 128, D)[:, perm, :].reshape(512, D)
    qwT = np.ascontiguousarray(
        q_perm.T.reshape(16, 128, 16, 128).transpose(2, 1, 0, 3))
    kwT = np.ascontiguousarray(
        k_perm.T.reshape(16, 128, 4, 128).transpose(2, 1, 0, 3))
    vwT = np.ascontiguousarray(
        kv_w[512:1024].T.reshape(16, 128, 512).transpose(1, 0, 2))
    cwT = np.ascontiguousarray(
        c_w.T.reshape(16, 128, D).transpose(1, 0, 2))
    rwT = np.ascontiguousarray(
        r_w.T.reshape(16, 128, 8).transpose(1, 0, 2))
    w1 = np.ascontiguousarray(w1.reshape(NE, 16, 128, DFF).transpose(0, 2, 1, 3))
    w2 = np.ascontiguousarray(w2.reshape(NE, 16, 128, DFF).transpose(0, 2, 1, 3))
    w3 = np.ascontiguousarray(w3.reshape(NE, 8, 128, D).transpose(0, 2, 1, 3))

    iden = np.eye(128, dtype=np.float32).astype(bf)
    ones = np.ones((128, 128), dtype=np.float32).astype(bf)
    kk = np.arange(128)[:, None, None]
    jq = np.arange(4)[None, :, None]
    qq = np.arange(512)[None, None, :]
    masks = (qq >= jq * 128 + kk).astype(np.float32).astype(bf)
    i64 = np.arange(64, dtype=np.float64)
    freqs = 1.0 / (10000.0 ** (2.0 * i64 / DK))
    tpos = np.arange(T, dtype=np.float64)
    ang = tpos[None, :] * freqs[:, None]
    ropeC = np.concatenate([np.cos(ang), np.cos(ang)], axis=0).astype(np.float32)
    ropeS = np.concatenate([-np.sin(ang), np.sin(ang)], axis=0).astype(np.float32)

    shared = {"qwT": qwT, "kwT": kwT, "vwT": vwT, "cwT": cwT, "rwT": rwT,
              "w1": w1, "w2": w2, "w3": w3, "iden": iden, "ones": ones,
              "masks": np.ascontiguousarray(masks), "ropeC": ropeC, "ropeS": ropeS}
    in_maps = []
    for b in range(8):
        m = dict(shared)
        m["x"] = np.ascontiguousarray(x[b])
        m["xT"] = np.ascontiguousarray(
            x[b].T.reshape(16, 128, T).transpose(1, 0, 2))
        in_maps.append(m)
    return in_maps


_tap_results = None
_STAGE = 4


def kernel(**inputs):
    global _tap_results
    in_maps = _host_inputs(**inputs)
    nc = build_program(tuple(_DEBUG), _STAGE)
    res = run_bass_kernel_spmd(nc, in_maps, core_ids=list(range(8)))
    if _DEBUG:
        _tap_results = [{k: v for k, v in r.items() if k.startswith("tap_")}
                        for r in res.results]
    return np.stack([np.asarray(res.results[b]["y"], dtype=np.float32)
                     for b in range(8)], axis=0)
